# version 08
# baseline (speedup 1.0000x reference)
import contextlib
import numpy as np
import ml_dtypes
import concourse.bass as bass
import concourse.mybir as mybir
from concourse.bass_utils import run_bass_kernel_spmd

F32 = mybir.dt.float32
BF16 = mybir.dt.bfloat16
I32 = mybir.dt.int32
ALU = mybir.AluOpType
ACT = mybir.ActivationFunctionType

D = 2048
SEQ = 8192
DIN = 4096
NST = 16
KCV = 4
HID = 5632
NC = 8
TC = SEQ // NC
TB = 512
EPS = 1e-5
BF = ml_dtypes.bfloat16

ENGS = ("sync", "vector", "scalar", "tensor", "gpsimd")


class Prog:
    def __init__(self, nc, stack, extra=()):
        self.nc = nc
        self.stack = stack
        self.ev = {k: [] for k in ENGS}
        self.sem = {}
        self.ctr = {}
        for name in ("te", "ve", "ac") + tuple(extra):
            self.sem[name] = stack.enter_context(nc.semaphore("s_" + name))
            self.ctr[name] = 0

    def emit(self, eng, fn, inc=None, dma=False):
        amt = 16 if dma else 1
        if inc is not None:
            self.ctr[inc] += amt
        sem = self.sem[inc] if inc is not None else None

        def run(e, fn=fn, sem=sem, amt=amt):
            inst = fn(e)
            if sem is not None:
                inst.then_inc(sem, amt)
        self.ev[eng].append(run)
        return self.ctr[inc] if inc is not None else None

    def wait(self, eng, name, val):
        if val <= 0:
            return
        sem = self.sem[name]
        self.ev[eng].append(lambda e, s=sem, v=val: e.wait_ge(s, v))

    def barrier(self):
        snap = dict(self.ctr)
        for eng in ENGS:
            for s, v in snap.items():
                self.wait(eng, s, v)

    def replay(self):
        with self.nc.Block() as blk:
            blk.sync(lambda e: [f(e) for f in self.ev["sync"]])
            blk.vector(lambda e: [f(e) for f in self.ev["vector"]])
            blk.scalar(lambda e: [f(e) for f in self.ev["scalar"]])
            blk.tensor(lambda e: [f(e) for f in self.ev["tensor"]])
            blk.gpsimd(lambda e: [f(e) for f in self.ev["gpsimd"]])


def build_fused():
    W = 4096
    POFF = SEQ - TC
    nc = bass.Bass("TRN2", target_bir_lowering=False, debug=False,
                   detect_race_conditions=False)
    xTf = nc.dram_tensor("xTf", [128, 16, SEQ], F32, kind="ExternalInput").ap()
    w_inx = nc.dram_tensor("w_inx", [16, 128, 16, 256], F32, kind="ExternalInput").ap()
    convw = nc.dram_tensor("convw", [128, 32, KCV], F32, kind="ExternalInput").ap()
    w_xp = nc.dram_tensor("w_xp", [128, 32, 160], F32, kind="ExternalInput").ap()
    w_dt = nc.dram_tensor("w_dt", [128, DIN], F32, kind="ExternalInput").ap()
    dt_b = nc.dram_tensor("dt_b", [128, 32], F32, kind="ExternalInput").ap()
    alog_i = nc.dram_tensor("alog_i", [128, 32, NST], F32, kind="ExternalInput").ap()
    w_inz = nc.dram_tensor("w_inz", [32, 128, 16, 128], F32, kind="ExternalInput").ap()
    w_out = nc.dram_tensor("w_out", [16, 128, 32, 128], F32, kind="ExternalInput").ap()
    wq_g = nc.dram_tensor("wq_g", [44, 128, 16, 128], I32, kind="ExternalInput").ap()
    wq_u = nc.dram_tensor("wq_u", [44, 128, 16, 128], I32, kind="ExternalInput").ap()
    wq_d = nc.dram_tensor("wq_d", [16, 128, 44, 128], I32, kind="ExternalInput").ap()
    s_g = nc.dram_tensor("s_g", [128, 44], F32, kind="ExternalInput").ap()
    s_u = nc.dram_tensor("s_u", [128, 44], F32, kind="ExternalInput").ap()
    s_d = nc.dram_tensor("s_d", [128, 16], F32, kind="ExternalInput").ap()
    dpar = nc.dram_tensor("dpar", [128, 32], F32, kind="ExternalInput").ap()
    o_T = nc.dram_tensor("o_T", [128, 16, TC], F32, kind="ExternalOutput").ap()

    wq_img = nc.dram_tensor("wq_img", [4, 128, 16, 1024], BF16, kind="Internal").ap()
    xn_sp = nc.dram_tensor("xn_sp", [128, 16, SEQ], BF16, kind="Internal").ap()
    xc_sp = nc.dram_tensor("xc_sp", [128, 32, SEQ], BF16, kind="Internal").ap()
    d_sp = nc.dram_tensor("d_sp", [128, 32, SEQ], F32, kind="Internal").ap()
    u_sp = nc.dram_tensor("u_sp", [128, 32, SEQ], BF16, kind="Internal").ap()
    b_sp = nc.dram_tensor("b_sp", [16, SEQ], F32, kind="Internal").ap()
    c_sp = nc.dram_tensor("c_sp", [16, SEQ], F32, kind="Internal").ap()
    bimg = nc.dram_tensor("bimg", [16, SEQ], BF16, kind="Internal").ap()
    cimg = nc.dram_tensor("cimg", [16, SEQ], BF16, kind="Internal").ap()
    y_sp = nc.dram_tensor("y_sp", [128, 32, TC], F32, kind="Internal").ap()
    x1sp = nc.dram_tensor("x1sp", [2, 128, 16, TB], F32, kind="Internal").ap()
    srow = nc.dram_tensor("srow", [2, TB], F32, kind="Internal").ap()

    stack = contextlib.ExitStack()
    with stack:
        psf = lambda n, s: stack.enter_context(nc.psum_tensor(n, s, F32))
        pA = [psf("pA0", [128, TB]), psf("pA1", [128, TB])]
        pst0 = psf("pst0", [1, TB])
        pst1 = psf("pst1", [1, TB])
        px2 = [psf("px0", [128, TB]), psf("px1", [128, TB])]
        onesb = stack.enter_context(nc.sbuf_tensor("onesb", [128, 1], BF16))
        onesf = stack.enter_context(nc.sbuf_tensor("onesf", [128, 1], F32))

        p = Prog(nc, stack, extra=(
            "ws", "sw", "xt", "gst", "gxn", "xa", "gxc", "xc0", "xc1",
            "gout", "gw", "gi", "d1", "u1", "r0", "r1", "gy", "gu",
            "f0", "f1", "i0", "i1", "xr", "yx", "gx1", "go"))

        p.emit("vector", lambda e: e.memset(onesb[:], 1.0), inc="ve")
        p.emit("vector", lambda e: e.memset(onesf[:], 1.0), inc="ve")

        with contextlib.ExitStack() as ph:
            sb = lambda n, s, d: ph.enter_context(nc.sbuf_tensor(n, s, d))
            wq4 = sb("wq4", [128, 16, 1024], BF16)
            fscr = sb("fscr", [128, 5120], F32)
            xpart = sb("xpart", [128, 16, TB + 8], F32)
            xnb = sb("xnb", [128, 16, TB], BF16)
            xcb = sb("xcb", [128, 8, TB], BF16)
            xcld = sb("xcld", [128, 2, TB], BF16)
            xcv2 = [sb("xcv0", [128, TB], F32), sb("xcv1", [128, TB], F32)]
            xdbl = sb("xdbl", [128, 2, TC], F32)
            dtr = sb("dtr", [128, TC], BF16)
            wxp = sb("wxp", [128, 32, 160], BF16)
            wdt = sb("wdt", [128, DIN], BF16)
            cw = sb("cw", [128, 32, KCV], F32)
            dtb = sb("dtb", [128, 32], F32)
            stat = sb("stat", [1, 6, TB], F32)
            mrep = sb("mrep", [128, TB], F32)
            rrep = sb("rrep", [128, TB], F32)
            dsc0 = [sb("dsc0a", [128, TB], F32), sb("dsc0b", [128, TB], F32)]
            dsc1 = [sb("dsc1a", [128, TB], F32), sb("dsc1b", [128, TB], F32)]
            dblk = [sb("dblka", [128, TB], F32), sb("dblkb", [128, TB], F32)]
            ublk = [sb("ublka", [128, TB], BF16), sb("ublkb", [128, TB], BF16)]
            xhalo = sb("xhalo", [128, 32, 3], F32)
            xh2 = sb("xh2", [128, 8, 3], F32)

            p.emit("vector", lambda e: e.memset(xhalo[:], 0.0), inc="ve")
            p.emit("sync", lambda e: e.dma_start(
                out=fscr[:, 0:5120].rearrange("p (m e) -> p m e", m=32), in_=w_xp), inc="ws", dma=True)
            mw0 = p.ctr["ws"]
            p.wait("vector", "ws", mw0)
            p.emit("vector", lambda e: e.tensor_copy(
                out=wxp[:], in_=fscr[:, 0:5120].rearrange("p (m e) -> p m e", m=32)), inc="ve")
            p.wait("sync", "ve", p.ctr["ve"])
            p.emit("sync", lambda e: e.dma_start(out=fscr[:, 0:4096], in_=w_dt), inc="ws", dma=True)
            mw1 = p.ctr["ws"]
            p.wait("vector", "ws", mw1)
            p.emit("vector", lambda e: e.tensor_copy(out=wdt[:], in_=fscr[:, 0:4096]), inc="ve")
            mwdt = p.ctr["ve"]
            p.emit("sync", lambda e: e.dma_start(out=cw[:], in_=convw), inc="sw", dma=True)
            p.emit("sync", lambda e: e.dma_start(out=dtb[:], in_=dt_b), inc="sw", dma=True)
            msw = p.ctr["sw"]
            p.wait("vector", "sw", msw)
            p.wait("scalar", "sw", msw)

            m_wcast = mwdt
            m_fscr_te = 0
            m_wq = ("ve", 0)
            m_wq_h = [("ws", 0), ("ws", 0)]
            m_half_free = [0, 0]
            m_wq_spill = 0
            m_xn_used = 0
            m_xn_hbm = 0
            m_conv_done = 0
            m_norm_done = 0
            m_stats_used = 0
            evac_marks = [0] * 8
            sil_marks = [0, 0]
            m_xc_hbm = 0
            m_xproj_te = 0
            m_xdbl_ve = 0
            m_px_ve = [0, 0]
            m_px_ac = [0, 0]
            m_dsc_free = [0, 0]
            m_u_slot = [0, 0]
            m_store_slot = [0, 0]
            m_dt_te = 0
            m_bc = 0

            for b in range(8):
                base = b * TC
                for (c0, w) in ((0, TB), (TB, TB)):
                    p.wait("sync", "ve", max(m_norm_done, m_conv_done))
                    p.emit("sync", lambda e, c0=c0, w=w, base=base: e.dma_start(
                        out=xpart[:, 0:8, 0:w], in_=xTf[:, 0:8, base + c0:base + c0 + w]), inc="xt", dma=True)
                    p.emit("sync", lambda e, c0=c0, w=w, base=base: e.dma_start(
                        out=xpart[:, 8:16, 0:w], in_=xTf[:, 8:16, base + c0:base + c0 + w]), inc="xt", dma=True)
                    mldx = p.ctr["xt"]
                    p.wait("tensor", "xt", mldx)
                    p.wait("tensor", "ve", m_stats_used)
                    for k in range(16):
                        p.emit("tensor", lambda e, k=k, w=w: e.matmul(
                            out=pst0[0:1, 0:w], lhsT=onesf[:], rhs=xpart[:, k, 0:w],
                            start=(k == 0), stop=(k == 15)),
                            inc="te" if (k == 15) else None)
                    fv = fscr[:, 0:4096].rearrange("p (a c) -> p a c", a=16)
                    for sp2 in range(2):
                        c2 = 256 * sp2
                        p.wait("scalar", "xt", mldx)
                        p.wait("scalar", "ve", m_wcast)
                        p.wait("scalar", "te", m_fscr_te)
                        p.emit("scalar", lambda e, c2=c2, fv=fv, w=w: e.activation(
                            out=fv, in_=xpart[:, :, c2:c2 + 256], func=ACT.Square), inc="ac")
                        msq = p.ctr["ac"]
                        p.wait("tensor", "ac", msq)
                        for k in range(16):
                            p.emit("tensor", lambda e, k=k, c2=c2, fv=fv: e.matmul(
                                out=pst1[0:1, c2:c2 + 256], lhsT=onesf[:], rhs=fv[:, k, :],
                                start=(k == 0), stop=(k == 15)),
                                inc="te" if (k == 15) else None)
                        m_fscr_te = p.ctr["te"]
                    mst = p.ctr["te"]
                    p.wait("vector", "te", mst)
                    p.emit("vector", lambda e, w=w: e.tensor_scalar_mul(out=stat[0:1, 0, 0:w], in0=pst0[0:1, 0:w], scalar1=1.0 / D))
                    p.emit("vector", lambda e, w=w: e.tensor_scalar_mul(out=stat[0:1, 1, 0:w], in0=pst1[0:1, 0:w], scalar1=1.0 / D))
                    p.emit("vector", lambda e, w=w: e.tensor_tensor(out=stat[0:1, 2, 0:w], in0=stat[0:1, 0, 0:w], in1=stat[0:1, 0, 0:w], op=ALU.mult))
                    p.emit("vector", lambda e, w=w: e.tensor_tensor(out=stat[0:1, 3, 0:w], in0=stat[0:1, 1, 0:w], in1=stat[0:1, 2, 0:w], op=ALU.subtract))
                    p.emit("vector", lambda e, w=w: e.tensor_scalar_add(out=stat[0:1, 3, 0:w], in0=stat[0:1, 3, 0:w], scalar1=EPS), inc="ve")
                    mvar = p.ctr["ve"]
                    m_stats_used = mvar
                    p.wait("scalar", "ve", mvar)
                    p.emit("scalar", lambda e, w=w: e.activation(out=stat[0:1, 4, 0:w], in_=stat[0:1, 3, 0:w], func=ACT.Sqrt), inc="ac")
                    msqr = p.ctr["ac"]
                    p.wait("vector", "ac", msqr)
                    p.emit("vector", lambda e, w=w: e.reciprocal(out=stat[0:1, 5, 0:w], in_=stat[0:1, 4, 0:w]), inc="ve")
                    mrs = p.ctr["ve"]
                    p.wait("gpsimd", "ve", mrs)
                    p.emit("gpsimd", lambda e, w=w: e.dma_start(out=srow[0:1, 0:w], in_=stat[0:1, 0, 0:w]), inc="gst", dma=True)
                    p.emit("gpsimd", lambda e, w=w: e.dma_start(out=srow[1:2, 0:w], in_=stat[0:1, 5, 0:w]), inc="gst", dma=True)
                    mb1 = p.ctr["gst"]
                    p.wait("gpsimd", "gst", mb1)
                    p.emit("gpsimd", lambda e, w=w: e.dma_start(out=mrep[:, 0:w], in_=srow[0:1, 0:w].partition_broadcast(128)), inc="gst", dma=True)
                    p.emit("gpsimd", lambda e, w=w: e.dma_start(out=rrep[:, 0:w], in_=srow[1:2, 0:w].partition_broadcast(128)), inc="gst", dma=True)
                    mb2 = p.ctr["gst"]
                    p.wait("vector", "gst", mb2)
                    p.wait("vector", "te", m_xn_used)
                    for k in range(16):
                        p.emit("vector", lambda e, k=k, w=w: e.tensor_tensor(
                            out=xpart[:, k, 0:w], in0=xpart[:, k, 0:w], in1=mrep[:, 0:w], op=ALU.subtract))
                        p.emit("vector", lambda e, k=k, w=w: e.tensor_tensor(
                            out=xnb[:, k, 0:w], in0=xpart[:, k, 0:w], in1=rrep[:, 0:w], op=ALU.mult))
                    p.emit("vector", lambda e: e.engine_nop(), inc="ve")
                    m_norm_done = p.ctr["ve"]
                    p.wait("gpsimd", "ve", m_norm_done)
                    p.emit("gpsimd", lambda e, c0=c0, w=w, base=base: e.dma_start(
                        out=xn_sp[:, 0:8, base + c0:base + c0 + w], in_=xnb[:, 0:8, 0:w]), inc="gxn", dma=True)
                    p.emit("gpsimd", lambda e, c0=c0, w=w, base=base: e.dma_start(
                        out=xn_sp[:, 8:16, base + c0:base + c0 + w], in_=xnb[:, 8:16, 0:w]), inc="gxn", dma=True)
                    m_xn_hbm = p.ctr["gxn"]

                for q in range(4):
                    if b == 0:
                        for g in range(4):
                            gj = 4 * q + g
                            p.wait("sync", "ve", m_wcast)
                            p.wait("sync", "te", m_fscr_te)
                            p.emit("sync", lambda e, gj=gj: e.dma_start(
                                out=fscr[:, 0:4096].rearrange("p (k c) -> p k c", k=16), in_=w_inx[gj]), inc="ws", dma=True)
                            mw = p.ctr["ws"]
                            p.wait("vector", "ws", mw)
                            if g == 0:
                                p.wait("vector", "te", m_xn_used)
                                p.wait("vector", "gw", m_wq_spill)
                            p.emit("vector", lambda e, g=g: e.tensor_copy(
                                out=wq4[:, :, 256 * g:256 * (g + 1)], in_=fscr[:, 0:4096].rearrange("p (k c) -> p k c", k=16)), inc="ve")
                            m_wcast = p.ctr["ve"]
                        m_wq = ("ve", m_wcast)
                        p.wait("gpsimd", "ve", m_wcast)
                        p.emit("gpsimd", lambda e, q=q: e.dma_start(
                            out=wq_img[q, :, 0:8, :], in_=wq4[:, 0:8, :]), inc="gw", dma=True)
                        p.emit("gpsimd", lambda e, q=q: e.dma_start(
                            out=wq_img[q, :, 8:16, :], in_=wq4[:, 8:16, :]), inc="gw", dma=True)
                        m_wq_spill = p.ctr["gw"]
                    else:
                        p.wait("sync", "gw", m_wq_spill)
                        p.wait("sync", "te", m_half_free[0])
                        p.emit("sync", lambda e, q=q: e.dma_start(
                            out=wq4[:, 0:8, 0:512], in_=wq_img[q, :, 0:8, 0:512]), inc="ws", dma=True)
                        p.emit("sync", lambda e, q=q: e.dma_start(
                            out=wq4[:, 8:16, 0:512], in_=wq_img[q, :, 8:16, 0:512]), inc="ws", dma=True)
                        mh0 = p.ctr["ws"]
                        p.wait("sync", "te", m_half_free[1])
                        p.emit("sync", lambda e, q=q: e.dma_start(
                            out=wq4[:, 0:8, 512:1024], in_=wq_img[q, :, 0:8, 512:1024]), inc="ws", dma=True)
                        p.emit("sync", lambda e, q=q: e.dma_start(
                            out=wq4[:, 8:16, 512:1024], in_=wq_img[q, :, 8:16, 512:1024]), inc="ws", dma=True)
                        m_wq_h = [("ws", mh0), ("ws", p.ctr["ws"])]
                    for b2 in range(2):
                        p.wait("sync", "gxn", m_xn_hbm)
                        p.wait("sync", "te", m_xn_used)
                        p.emit("sync", lambda e, b2=b2, base=base: e.dma_start(
                            out=xnb[:, 0:8, :], in_=xn_sp[:, 0:8, base + TB * b2:base + TB * b2 + TB]), inc="xa", dma=True)
                        p.emit("sync", lambda e, b2=b2, base=base: e.dma_start(
                            out=xnb[:, 8:16, :], in_=xn_sp[:, 8:16, base + TB * b2:base + TB * b2 + TB]), inc="xa", dma=True)
                        mxn = p.ctr["xa"]
                        if b == 0:
                            p.wait("tensor", m_wq[0], m_wq[1])
                        p.wait("tensor", "xa", mxn)
                        p.wait("tensor", "ve", m_conv_done)
                        for m in range(8):
                            sl = m % 2
                            if b > 0 and m % 4 == 0:
                                mwh = m_wq_h[m // 4]
                                p.wait("tensor", mwh[0], mwh[1])
                            if m >= 2:
                                p.wait("tensor", "ac", evac_marks[m - 2])
                            for k in range(16):
                                p.emit("tensor", lambda e, m=m, k=k, sl=sl: e.matmul(
                                    out=pA[sl][:, 0:TB], lhsT=wq4[:, k, 128 * m:128 * (m + 1)],
                                    rhs=xnb[:, k, :], start=(k == 0), stop=(k == 15)),
                                    inc="te" if k == 15 else None)
                            mte = p.ctr["te"]
                            if b2 == 1 and (m == 3 or m == 7):
                                m_half_free[m // 4] = mte
                            p.wait("scalar", "te", mte)
                            p.emit("scalar", lambda e, m=m, sl=sl: e.activation(
                                out=xpart[:, m, 3:TB + 3], in_=pA[sl][:, 0:TB], func=ACT.Identity), inc="ac")
                            evac_marks[m] = p.ctr["ac"]
                        m_xn_used = p.ctr["te"]
                        m_evac = p.ctr["ac"]
                        p.wait("vector", "ac", m_evac)
                        p.wait("vector", "te", m_xproj_te)
                        p.wait("vector", "gxc", m_xc_hbm)
                        for m in range(8):
                            mc = 8 * q + m
                            xv = xcv2[m % 2]
                            hsrc = (xhalo[:, mc, :] if b2 == 0 else xh2[:, m, :])
                            p.emit("vector", lambda e, m=m, hsrc=hsrc: e.tensor_copy(
                                out=xpart[:, m, 0:3], in_=hsrc))
                            p.wait("vector", "ac", sil_marks[m % 2])
                            p.emit("vector", lambda e, mc=mc, m=m, xv=xv: e.tensor_scalar_mul(
                                out=xv[:, :], in0=xpart[:, m, 0:TB], scalar1=cw[:, mc, 0:1]))
                            for k in range(1, KCV):
                                p.emit("vector", lambda e, m=m, mc=mc, k=k, xv=xv: e.scalar_tensor_tensor(
                                    out=xv[:, :], in0=xpart[:, m, k:TB + k], scalar=cw[:, mc, k:k + 1],
                                    in1=xv[:, :], op0=ALU.mult, op1=ALU.add), inc="ve" if k == KCV - 1 else None)
                            mcv = p.ctr["ve"]
                            hdst = (xh2[:, m, :] if b2 == 0 else xhalo[:, mc, :])
                            p.emit("vector", lambda e, m=m, hdst=hdst: e.tensor_copy(
                                out=hdst, in_=xpart[:, m, TB:TB + 3]))
                            p.wait("scalar", "ve", mcv)
                            p.emit("scalar", lambda e, m=m, xv=xv: e.activation(
                                out=xcb[:, m, :], in_=xv[:, :], func=ACT.Silu), inc="ac")
                            sil_marks[m % 2] = p.ctr["ac"]
                        p.emit("vector", lambda e: e.engine_nop(), inc="ve")
                        m_conv_done = p.ctr["ve"]
                        msil = p.ctr["ac"]
                        p.wait("gpsimd", "ac", msil)
                        p.emit("gpsimd", lambda e, q=q, b2=b2, base=base: e.dma_start(
                            out=xc_sp[:, 8 * q:8 * (q + 1), base + TB * b2:base + TB * (b2 + 1)], in_=xcb[:, :, :]),
                            inc="gxc", dma=True)
                        m_xc_hbm = p.ctr["gxc"]
                        p.wait("tensor", "ac", msil)
                        for mo, mwd in ((0, 128), (128, 32)):
                            pxs = 0 if mo == 0 else 1
                            p.wait("tensor", "ve", m_px_ve[pxs])
                            p.wait("tensor", "ac", m_px_ac[pxs])
                            for k in range(8):
                                p.emit("tensor", lambda e, mo=mo, mwd=mwd, k=k, q=q, pxs=pxs: e.matmul(
                                    out=px2[pxs][0:mwd, 0:TB], lhsT=wxp[:, 8 * q + k, mo:mo + mwd],
                                    rhs=xcb[:, k, :], start=(k == 0), stop=(k == 7)),
                                    inc="te" if k == 7 else None)
                            mxp = p.ctr["te"]
                            p.wait("vector", "te", mxp)
                            ch = 0 if mo == 0 else 1
                            if q == 0:
                                p.wait("vector", "gout", m_bc)
                                p.emit("vector", lambda e, ch=ch, mwd=mwd, b2=b2, pxs=pxs: e.tensor_copy(
                                    out=xdbl[0:mwd, ch, TB * b2:TB * (b2 + 1)], in_=px2[pxs][0:mwd, 0:TB]), inc="ve")
                            else:
                                p.emit("vector", lambda e, ch=ch, mwd=mwd, b2=b2, pxs=pxs: e.tensor_tensor(
                                    out=xdbl[0:mwd, ch, TB * b2:TB * (b2 + 1)], in0=xdbl[0:mwd, ch, TB * b2:TB * (b2 + 1)],
                                    in1=px2[pxs][0:mwd, 0:TB], op=ALU.add), inc="ve")
                            m_xdbl_ve = p.ctr["ve"]
                            m_px_ve[pxs] = p.ctr["ve"]
                        m_xproj_te = p.ctr["te"]

                p.wait("vector", "ve", m_xdbl_ve)
                p.wait("vector", "te", m_dt_te)
                p.emit("vector", lambda e: e.tensor_copy(out=dtr[:], in_=xdbl[:, 0, :]), inc="ve")
                mdtr = p.ctr["ve"]
                p.wait("gpsimd", "ve", mdtr)
                p.emit("gpsimd", lambda e, base=base: e.dma_start(
                    out=b_sp[:, base:base + TC], in_=xdbl[0:16, 1, :]), inc="gout", dma=True)
                p.emit("gpsimd", lambda e, base=base: e.dma_start(
                    out=c_sp[:, base:base + TC], in_=xdbl[16:32, 1, :]), inc="gout", dma=True)
                m_bc = p.ctr["gout"]

                for m in range(32):
                    for b2 in range(2):
                        par = (m * 2 + b2) % 2
                        xcsem = "xc0" if par == 0 else "xc1"
                        p.wait("sync", "gxc", m_xc_hbm)
                        p.wait("sync", "ve", m_u_slot[par])
                        p.emit("sync", lambda e, m=m, b2=b2, par=par, base=base: e.dma_start(
                            out=xcld[:, par, :], in_=xc_sp[:, m, base + TB * b2:base + TB * (b2 + 1)]), inc=xcsem, dma=True)
                        mxc = p.ctr[xcsem]
                        p.wait("tensor", "ve", mdtr)
                        p.wait("tensor", "ac", m_px_ac[par])
                        p.wait("tensor", "ve", m_px_ve[par])
                        p.emit("tensor", lambda e, m=m, b2=b2, par=par: e.matmul(
                            out=px2[par][:, 0:TB], lhsT=wdt[:, 128 * m:128 * (m + 1)],
                            rhs=dtr[:, TB * b2:TB * (b2 + 1)], start=True, stop=True), inc="te")
                        mdt = p.ctr["te"]
                        p.wait("scalar", "te", mdt)
                        p.wait("scalar", "ve", m_dsc_free[par])
                        p.emit("scalar", lambda e, m=m, par=par: e.activation(
                            out=dsc0[par][:, :], in_=px2[par][:, 0:TB], func=ACT.Abs, bias=dtb[:, m:m + 1]))
                        p.emit("scalar", lambda e, m=m, par=par: e.activation(
                            out=dsc1[par][:, :], in_=px2[par][:, 0:TB], func=ACT.Relu, bias=dtb[:, m:m + 1]), inc="ac")
                        m_px_ac[par] = p.ctr["ac"]
                        p.emit("scalar", lambda e, par=par: e.activation(
                            out=dsc0[par][:, :], in_=dsc0[par][:, :], func=ACT.Exp, scale=-1.0))
                        p.emit("scalar", lambda e, par=par: e.activation(
                            out=dsc0[par][:, :], in_=dsc0[par][:, :], func=ACT.Ln, bias=1.0), inc="ac")
                        mln = p.ctr["ac"]
                        p.wait("vector", "ac", mln)
                        p.wait("vector", "gout", m_store_slot[par])
                        p.emit("vector", lambda e, par=par: e.tensor_tensor(
                            out=dblk[par][:, :], in0=dsc0[par][:, :], in1=dsc1[par][:, :], op=ALU.add), inc="ve")
                        p.wait("vector", xcsem, mxc)
                        p.emit("vector", lambda e, par=par: e.tensor_tensor(
                            out=ublk[par][:, :], in0=dblk[par][:, :], in1=xcld[:, par, :], op=ALU.mult), inc="ve")
                        m_u_slot[par] = p.ctr["ve"]
                        m_dsc_free[par] = p.ctr["ve"]
                        p.wait("gpsimd", "ve", m_u_slot[par])
                        p.emit("gpsimd", lambda e, m=m, b2=b2, par=par, base=base: e.dma_start(
                            out=d_sp[:, m, base + TB * b2:base + TB * (b2 + 1)], in_=dblk[par][:, :]), inc="gout", dma=True)
                        p.emit("gpsimd", lambda e, m=m, b2=b2, par=par, base=base: e.dma_start(
                            out=u_sp[:, m, base + TB * b2:base + TB * (b2 + 1)], in_=ublk[par][:, :]), inc="gout", dma=True)
                        m_store_slot[par] = p.ctr["gout"]
                m_dt_te = p.ctr["te"]

            p.barrier()

        with contextlib.ExitStack() as ph:
            sb = lambda n, s, d: ph.enter_context(nc.sbuf_tensor(n, s, d))
            dl = [[sb("dl00", [128, W], F32), sb("dl01", [128, W], F32)],
                  [sb("dl10", [128, W], F32), sb("dl11", [128, W], F32)]]
            uu = [sb("uu0", [128, W], BF16), sb("uu1", [128, W], BF16)]
            dA2 = [sb("dA0", [128, W], F32), sb("dA1", [128, W], F32)]
            uB2 = [sb("uB0", [128, W], BF16), sb("uB1", [128, W], BF16)]
            h1b = sb("h1b", [128, W], BF16)
            brep = [sb("brep0", [128, W], BF16), sb("brep1", [128, W], BF16)]
            crep = [sb("crep0", [128, TC], BF16), sb("crep1", [128, TC], BF16)]
            trees = []
            for mi in range(2):
                trees.append({
                    "hc0": sb(f"hc0_{mi}", [128, TC], BF16),
                    "hc1": sb(f"hc1_{mi}", [128, TC], BF16),
                    "s2b": sb(f"s2b_{mi}", [128, TC], BF16),
                    "s4b": sb(f"s4b_{mi}", [128, TC], BF16),
                    "s8b": sb(f"s8b_{mi}", [128, TC], BF16),
                    "yb": sb(f"yb_{mi}", [128, TC], F32),
                })
            bcs = sb("bcs", [16, 1024], F32)
            bcb = sb("bcb", [16, 1024], BF16)
            aval = sb("aval", [128, 32, NST], F32)
            state = sb("state", [128, 32, NST], F32)

            p.emit("sync", lambda e: e.dma_start(out=aval[:], in_=alog_i), inc="sw", dma=True)
            ma = p.ctr["sw"]
            p.wait("scalar", "sw", ma)
            p.emit("scalar", lambda e: e.activation(out=aval[:], in_=aval[:], func=ACT.Exp), inc="ac")
            mae = p.ctr["ac"]
            p.wait("vector", "ac", mae)
            p.emit("vector", lambda e: e.tensor_scalar_mul(out=aval[:], in0=aval[:], scalar1=-1.0), inc="ve")
            mave = p.ctr["ve"]
            for src, dst in ((b_sp, bimg), (c_sp, cimg)):
                for hh in range(8):
                    p.wait("sync", "gi", p.ctr["gi"])
                    p.emit("sync", lambda e, src=src, hh=hh: e.dma_start(
                        out=bcs[:], in_=src[:, 1024 * hh:1024 * (hh + 1)]), inc="sw", dma=True)
                    p.wait("vector", "sw", p.ctr["sw"])
                    p.emit("vector", lambda e: e.tensor_copy(out=bcb[:], in_=bcs[:]), inc="ve")
                    p.wait("gpsimd", "ve", p.ctr["ve"])
                    p.emit("gpsimd", lambda e, dst=dst, hh=hh: e.dma_start(
                        out=dst[:, 1024 * hh:1024 * (hh + 1)], in_=bcb[:]), inc="gi", dma=True)
            m_img = p.ctr["gi"]

            m_dl_used = [0, 0]
            m_uu_used = 0
            m_dA_used = [0, 0]
            m_uB_used = [0, 0]
            m_crep_used = [0, 0]
            m_y = [0, 0]
            m_y_done = [0, 0]
            it2 = 0
            for mb in range(16):
                for w in range(2):
                    woff = W * w
                    slot = (2 * mb + w) % 2
                    p.wait("sync", "ac", m_dl_used[slot])
                    for mi in range(2):
                        m = 2 * mb + mi
                        p.emit("sync", lambda e, m=m, slot=slot, mi=mi, woff=woff: e.dma_start(
                            out=dl[slot][mi][:], in_=d_sp[:, m, woff:woff + W]), inc="d1", dma=True)
                    mdl = p.ctr["d1"]
                    p.wait("sync", "gu", m_uu_used)
                    for mi in range(2):
                        m = 2 * mb + mi
                        p.emit("sync", lambda e, m=m, mi=mi, woff=woff: e.dma_start(
                            out=uu[mi][:], in_=u_sp[:, m, woff:woff + W]), inc="u1", dma=True)
                    muu = p.ctr["u1"]
                    for n in range(NST):
                        rs = n % 2
                        rsem = "r0" if rs == 0 else "r1"
                        p.wait("gpsimd", "gi", m_img)
                        p.wait("gpsimd", "ve", m_crep_used[rs])
                        p.emit("gpsimd", lambda e, n=n, woff=woff, rs=rs: e.dma_start(
                            out=brep[rs][:], in_=bimg[n:n + 1, woff:woff + W].partition_broadcast(128)), inc=rsem, dma=True)
                        if w == 1:
                            p.emit("gpsimd", lambda e, n=n, woff=woff, rs=rs: e.dma_start(
                                out=crep[rs][:], in_=cimg[n:n + 1, woff + 3072:woff + W].partition_broadcast(128)), inc=rsem, dma=True)
                        mrep_ = p.ctr[rsem]
                        for mi in range(2):
                            m = 2 * mb + mi
                            up = it2 % 2
                            it2 += 1
                            p.wait("scalar", "d1", mdl)
                            p.wait("scalar", "ve", max(m_dA_used[mi], mave))
                            p.emit("scalar", lambda e, m=m, mi=mi, slot=slot: e.activation(
                                out=dA2[mi][:], in_=dl[slot][mi][:], func=ACT.Exp, scale=aval[:, m, n:n + 1]), inc="ac")
                            mexp = p.ctr["ac"]
                            if n == NST - 1 and mi == 1:
                                m_dl_used[slot] = mexp
                            p.wait("gpsimd", "u1", muu)
                            p.wait("gpsimd", rsem, mrep_)
                            p.wait("gpsimd", "ve", m_uB_used[up])
                            p.emit("gpsimd", lambda e, mi=mi, rs=rs, up=up: e.tensor_tensor(
                                out=uB2[up][:], in0=uu[mi][:], in1=brep[rs][:], op=ALU.mult), inc="gu")
                            muB = p.ctr["gu"]
                            if n == NST - 1 and mi == 1:
                                m_uu_used = muB
                            p.wait("vector", "ac", mexp)
                            p.wait("vector", "gu", muB)
                            if w == 0:
                                p.emit("vector", lambda e, m=m, n=n, mi=mi, up=up: e.tensor_tensor_scan(
                                    out=h1b[:], data0=dA2[mi][:], data1=uB2[up][:], initial=0.0,
                                    op0=ALU.mult, op1=ALU.add), inc="ve")
                                m_dA_used[mi] = p.ctr["ve"]
                                m_uB_used[up] = p.ctr["ve"]
                                p.emit("vector", lambda e, m=m, n=n: e.tensor_copy(
                                    out=state[:, m, n:n + 1], in_=h1b[:, W - 1:W]))
                            else:
                                p.emit("vector", lambda e, m=m, n=n, mi=mi, up=up: e.tensor_tensor_scan(
                                    out=h1b[:], data0=dA2[mi][:], data1=uB2[up][:], initial=state[:, m, n:n + 1],
                                    op0=ALU.mult, op1=ALU.add), inc="ve")
                                m_dA_used[mi] = p.ctr["ve"]
                                m_uB_used[up] = p.ctr["ve"]
                                T = trees[mi]
                                hcp = (T["hc0"], T["hc1"])[n % 2]
                                p.emit("vector", lambda e, hcp=hcp, rs=rs: e.tensor_tensor(
                                    out=hcp[:], in0=h1b[:, 3072:W], in1=crep[rs][:], op=ALU.mult), inc="ve")
                                if n % 2 == 1:
                                    if (n // 2) % 2 == 0:
                                        p.emit("vector", lambda e, T=T: e.tensor_tensor(
                                            out=T["s2b"][:], in0=T["hc0"][:], in1=T["hc1"][:], op=ALU.add))
                                    else:
                                        p.emit("vector", lambda e, T=T: e.tensor_tensor(
                                            out=T["hc0"][:], in0=T["hc0"][:], in1=T["hc1"][:], op=ALU.add))
                                        if (n // 4) % 2 == 0:
                                            p.emit("vector", lambda e, T=T: e.tensor_tensor(
                                                out=T["s4b"][:], in0=T["s2b"][:], in1=T["hc0"][:], op=ALU.add))
                                        else:
                                            p.emit("vector", lambda e, T=T: e.tensor_tensor(
                                                out=T["hc0"][:], in0=T["s2b"][:], in1=T["hc0"][:], op=ALU.add))
                                            if n == 7:
                                                p.emit("vector", lambda e, T=T: e.tensor_tensor(
                                                    out=T["s8b"][:], in0=T["s4b"][:], in1=T["hc0"][:], op=ALU.add))
                                            else:
                                                p.wait("vector", "gy", m_y_done[mi])
                                                p.emit("vector", lambda e, T=T: e.tensor_tensor(
                                                    out=T["hc0"][:], in0=T["s4b"][:], in1=T["hc0"][:], op=ALU.add))
                                                p.emit("vector", lambda e, T=T: e.tensor_tensor(
                                                    out=T["yb"][:], in0=T["s8b"][:], in1=T["hc0"][:], op=ALU.add), inc="ve")
                                                m_y[mi] = p.ctr["ve"]
                            m_crep_used[rs] = p.ctr["ve"]
                    if w == 1:
                        for mi in range(2):
                            m = 2 * mb + mi
                            p.wait("gpsimd", "ve", m_y[mi])
                            p.emit("gpsimd", lambda e, m=m, mi=mi: e.dma_start(
                                out=y_sp[:, m, :], in_=trees[mi]["yb"][:]), inc="gy", dma=True)
                            m_y_done[mi] = p.ctr["gy"]

            p.barrier()

        TH = TB
        with contextlib.ExitStack() as ph:
            sb = lambda n, s, d: ph.enter_context(nc.sbuf_tensor(n, s, d))
            area = sb("area", [128, 44, TH], BF16)
            xn = sb("p_xn", [128, 16, TH], BF16)
            xn2 = xn
            x1 = sb("p_x1", [128, 16, TH], F32)
            fstg = [sb("fstg0", [128, 32, 128], F32), sb("fstg1", [128, 32, 128], F32)]
            istg = [sb("istg0", [128, 16, 128], I32), sb("istg1", [128, 16, 128], I32)]
            wt = [sb("wt0", [128, 44, 128], BF16), sb("wt1", [128, 44, 128], BF16)]
            ytl = sb("ytl", [128, TH], F32)
            gtmp = sb("gtmp", [128, TH], F32)
            gtmpb = sb("gtmpb", [128, TH], F32)
            gt = sb("gt", [128, TH], BF16)
            ut = sb("ut", [128, TH], BF16)
            xct = sb("xct", [128, TH], BF16)
            pstat = sb("p_stat", [1, 4, TH], F32)
            pmrep = sb("p_mrep", [128, TH], F32)
            prrep = sb("p_rrep", [128, TH], F32)
            cs_rep = sb("cs_rep", [128, TH], F32)
            cs2_rep = sb("cs2_rep", [128, TH], F32)
            sgv = sb("sgv", [128, 44], F32)
            suv = sb("suv", [128, 44], F32)
            sdv = sb("sdv", [128, 16], F32)
            sd128 = sb("sd128", [128, 16], F32)
            dp = sb("dp", [128, 32], F32)
            x1r = sb("x1r", [128, TH], F32)
            otile = sb("otile", [128, TH], F32)

            p.emit("sync", lambda e: e.dma_start(out=sgv[:], in_=s_g), inc="sw", dma=True)
            p.emit("sync", lambda e: e.dma_start(out=suv[:], in_=s_u), inc="sw", dma=True)
            p.emit("sync", lambda e: e.dma_start(out=sdv[:], in_=s_d), inc="sw", dma=True)
            p.emit("sync", lambda e: e.dma_start(out=dp[:], in_=dpar), inc="sw", dma=True)
            msetup = p.ctr["sw"]
            p.wait("vector", "sw", msetup)
            p.emit("vector", lambda e: e.tensor_scalar_mul(out=sd128[:], in0=sdv[:], scalar1=-128.0), inc="ve")
            msd = p.ctr["ve"]
            p.wait("scalar", "sw", msetup)

            m_cast = [0, 0]
            m_wused = [0, 0]
            m_evA = [0, 0]
            m_evV = [0, 0]
            m_xn_used = 0
            m_out_done = 0
            m_stats_used = 0
            m_x1r_used = 0
            m_ytl_used = 0
            m_x1sp = 0

            for th in range(2):
                t0 = TH * th
                foff = POFF + TH * th
                p.wait("sync", "te", m_xn_used)
                p.emit("sync", lambda e, foff=foff: e.dma_start(
                    out=xn[:, 0:8, :], in_=xn_sp[:, 0:8, foff:foff + TH]), inc="xa", dma=True)
                p.emit("sync", lambda e, foff=foff: e.dma_start(
                    out=xn[:, 8:16, :], in_=xn_sp[:, 8:16, foff:foff + TH]), inc="xa", dma=True)
                mxn = p.ctr["xa"]
                def z_load(m):
                    sl = m % 2
                    fsem = "f0" if sl == 0 else "f1"
                    p.wait("sync", "ve", m_cast[sl])
                    p.emit("sync", lambda e, m=m, sl=sl: e.dma_start(
                        out=fstg[sl][:, 0:16, :], in_=w_inz[m]), inc=fsem, dma=True)
                    mwl = p.ctr[fsem]
                    p.wait("vector", fsem, mwl)
                    p.wait("vector", "te", m_wused[sl])
                    p.emit("vector", lambda e, sl=sl: e.tensor_copy(out=wt[sl][:, 0:16, :], in_=fstg[sl][:, 0:16, :]), inc="ve")
                    m_cast[sl] = p.ctr["ve"]
                    castm[m] = p.ctr["ve"]
                castm = {}
                z_load(0)
                for m in range(32):
                    sl = m % 2
                    if m + 1 < 32:
                        z_load(m + 1)
                    p.wait("tensor", "ve", castm[m])
                    p.wait("tensor", "xa", mxn)
                    p.wait("tensor", "ac", m_evA[sl])
                    p.wait("tensor", "ve", m_evV[sl])
                    for k in range(16):
                        p.emit("tensor", lambda e, k=k, sl=sl: e.matmul(
                            out=pA[sl][:, :], lhsT=wt[sl][:, k, :], rhs=xn[:, k, :],
                            start=(k == 0), stop=(k == 15)), inc="te" if k == 15 else None)
                    m_wused[sl] = p.ctr["te"]
                    p.wait("scalar", "te", m_wused[sl])
                    p.emit("scalar", lambda e, m=m, sl=sl: e.activation(
                        out=area[:, m, :], in_=pA[sl][:, :], func=ACT.Silu), inc="ac")
                    m_evA[sl] = p.ctr["ac"]
                msz = p.ctr["ac"]
                m_xn_used = p.ctr["te"]
                for m in range(32):
                    p.wait("sync", "ve", m_ytl_used)
                    p.emit("sync", lambda e, m=m, t0=t0: e.dma_start(
                        out=ytl[:, :], in_=y_sp[:, m, t0:t0 + TH]), inc="yx", dma=True)
                    p.emit("sync", lambda e, m=m, foff=foff: e.dma_start(
                        out=xct[:, :], in_=xc_sp[:, m, foff:foff + TH]), inc="yx", dma=True)
                    mxcl = p.ctr["yx"]
                    p.wait("vector", "yx", mxcl)
                    p.wait("vector", "ac", msz)
                    p.emit("vector", lambda e, m=m: e.scalar_tensor_tensor(
                        out=gtmp[:, :], in0=xct[:, :], scalar=dp[:, m:m + 1], in1=ytl[:, :],
                        op0=ALU.mult, op1=ALU.add), inc="ve")
                    m_ytl_used = p.ctr["ve"]
                    p.emit("vector", lambda e, m=m: e.tensor_tensor(
                        out=area[:, m, :], in0=gtmp[:, :], in1=area[:, m, :], op=ALU.mult), inc="ve")
                my2 = p.ctr["ve"]
                def op_load(m):
                    sl = m % 2
                    fsem = "f0" if sl == 0 else "f1"
                    p.wait("sync", "ve", m_cast[sl])
                    p.emit("sync", lambda e, m=m, sl=sl: e.dma_start(
                        out=fstg[sl][:], in_=w_out[m]), inc=fsem, dma=True)
                    mwl = p.ctr[fsem]
                    p.wait("vector", fsem, mwl)
                    p.wait("vector", "te", m_wused[sl])
                    p.emit("vector", lambda e, sl=sl: e.tensor_copy(out=wt[sl][:, 0:32, :], in_=fstg[sl][:]), inc="ve")
                    m_cast[sl] = p.ctr["ve"]
                    castm2[m] = p.ctr["ve"]
                castm2 = {}
                op_load(0)
                for m in range(16):
                    sl = m % 2
                    if m + 1 < 16:
                        op_load(m + 1)
                    p.wait("sync", "ve", m_x1r_used)
                    p.emit("sync", lambda e, m=m, foff=foff: e.dma_start(
                        out=x1r[:, :], in_=xTf[:, m, foff:foff + TH]), inc="xr", dma=True)
                    mxl = p.ctr["xr"]
                    p.wait("tensor", "ve", max(castm2[m], my2))
                    p.wait("tensor", "ac", m_evA[sl])
                    p.wait("tensor", "ve", m_evV[sl])
                    for k in range(32):
                        p.emit("tensor", lambda e, k=k, sl=sl: e.matmul(
                            out=pA[sl][:, :], lhsT=wt[sl][:, k, :], rhs=area[:, k, :],
                            start=(k == 0), stop=(k == 31)), inc="te" if k == 31 else None)
                    m_wused[sl] = p.ctr["te"]
                    p.wait("vector", "te", m_wused[sl])
                    p.wait("vector", "xr", mxl)
                    p.emit("vector", lambda e, m=m, sl=sl: e.tensor_tensor(
                        out=x1[:, m, :], in0=pA[sl][:, :], in1=x1r[:, :], op=ALU.add), inc="ve")
                    m_evV[sl] = p.ctr["ve"]
                    m_x1r_used = p.ctr["ve"]
                mx1 = p.ctr["ve"]
                p.wait("gpsimd", "ve", mx1)
                p.emit("gpsimd", lambda e, th=th: e.dma_start(
                    out=x1sp[th], in_=x1[:, :, :]), inc="gx1", dma=True)
                m_x1sp = p.ctr["gx1"]
                p.wait("tensor", "ve", max(mx1, m_stats_used))
                for k in range(16):
                    p.emit("tensor", lambda e, k=k: e.matmul(
                        out=pst0[0:1, 0:TH], lhsT=onesf[:], rhs=x1[:, k, :],
                        start=(k == 0), stop=(k == 15)), inc="te" if k == 15 else None)
                for sp2 in range(2):
                    c0 = 256 * sp2
                    sl = sp2 % 2
                    p.wait("scalar", "ve", max(mx1, m_cast[sl]))
                    p.emit("scalar", lambda e, c0=c0, sl=sl: e.activation(
                        out=fstg[sl][:].rearrange("p a b -> p (a b)").rearrange("p (a c) -> p a c", a=16),
                        in_=x1[:, :, c0:c0 + 256], func=ACT.Square), inc="ac")
                    msq2 = p.ctr["ac"]
                    p.wait("tensor", "ac", msq2)
                    for k in range(16):
                        p.emit("tensor", lambda e, k=k, c0=c0, sl=sl: e.matmul(
                            out=pst1[0:1, c0:c0 + 256],
                            lhsT=onesf[:],
                            rhs=fstg[sl][:].rearrange("p a b -> p (a b)").rearrange("p (a c) -> p a c", a=16)[:, k, :],
                            start=(k == 0), stop=(k == 15)), inc="te" if k == 15 else None)
                    m_wused[sl] = max(m_wused[sl], p.ctr["te"])
                msx2 = p.ctr["te"]
                p.wait("vector", "te", msx2)
                p.emit("vector", lambda e: e.tensor_scalar_mul(out=pstat[0:1, 0, :], in0=pst0[0:1, 0:TH], scalar1=1.0 / D))
                p.emit("vector", lambda e: e.tensor_scalar_mul(out=pstat[0:1, 1, :], in0=pst1[0:1, 0:TH], scalar1=1.0 / D))
                p.emit("vector", lambda e: e.tensor_tensor(out=pstat[0:1, 2, :], in0=pstat[0:1, 0, :], in1=pstat[0:1, 0, :], op=ALU.mult))
                p.emit("vector", lambda e: e.tensor_tensor(out=pstat[0:1, 2, :], in0=pstat[0:1, 1, :], in1=pstat[0:1, 2, :], op=ALU.subtract))
                p.emit("vector", lambda e: e.tensor_scalar_add(out=pstat[0:1, 2, :], in0=pstat[0:1, 2, :], scalar1=EPS), inc="ve")
                mvar = p.ctr["ve"]
                m_stats_used = mvar
                p.wait("scalar", "ve", mvar)
                p.emit("scalar", lambda e: e.activation(out=pstat[0:1, 2, :], in_=pstat[0:1, 2, :], func=ACT.Sqrt), inc="ac")
                msqr = p.ctr["ac"]
                p.wait("vector", "ac", msqr)
                p.emit("vector", lambda e: e.reciprocal(out=pstat[0:1, 2, :], in_=pstat[0:1, 2, :]), inc="ve")
                mrs = p.ctr["ve"]
                p.wait("gpsimd", "ve", mrs)
                p.emit("gpsimd", lambda e: e.dma_start(out=srow[0:1, :], in_=pstat[0:1, 0, :]), inc="gst", dma=True)
                p.emit("gpsimd", lambda e: e.dma_start(out=srow[1:2, :], in_=pstat[0:1, 2, :]), inc="gst", dma=True)
                mb1 = p.ctr["gst"]
                p.wait("gpsimd", "gst", mb1)
                p.emit("gpsimd", lambda e: e.dma_start(out=pmrep[:, :], in_=srow[0:1, :].partition_broadcast(128)), inc="gst", dma=True)
                p.emit("gpsimd", lambda e: e.dma_start(out=prrep[:, :], in_=srow[1:2, :].partition_broadcast(128)), inc="gst", dma=True)
                mb2 = p.ctr["gst"]
                p.wait("vector", "gst", mb2)
                p.wait("vector", "te", max(msx2, m_xn_used))
                for k in range(16):
                    p.emit("vector", lambda e, k=k: e.tensor_tensor(
                        out=x1[:, k, :], in0=x1[:, k, :], in1=pmrep[:, :], op=ALU.subtract))
                    p.emit("vector", lambda e, k=k: e.tensor_tensor(
                        out=xn2[:, k, :], in0=x1[:, k, :], in1=prrep[:, :], op=ALU.mult))
                p.emit("vector", lambda e: e.engine_nop(), inc="ve")
                mxn2 = p.ctr["ve"]
                p.wait("tensor", "ve", max(mxn2, m_stats_used))
                for k in range(16):
                    p.emit("tensor", lambda e, k=k: e.matmul(
                        out=pst0[0:1, 0:TH], lhsT=onesb[:], rhs=xn2[:, k, :],
                        start=(k == 0), stop=(k == 15)), inc="te" if k == 15 else None)
                mcs = p.ctr["te"]
                p.wait("vector", "te", mcs)
                p.emit("vector", lambda e: e.tensor_scalar_mul(out=pstat[0:1, 3, :], in0=pst0[0:1, 0:TH], scalar1=128.0), inc="ve")
                mcsr = p.ctr["ve"]
                m_stats_used = mcsr
                p.wait("gpsimd", "ve", mcsr)
                p.emit("gpsimd", lambda e: e.dma_start(out=srow[0:1, :], in_=pstat[0:1, 3, :]), inc="gst", dma=True)
                mb3 = p.ctr["gst"]
                p.wait("gpsimd", "gst", mb3)
                p.emit("gpsimd", lambda e: e.dma_start(out=cs_rep[:, :], in_=srow[0:1, :].partition_broadcast(128)), inc="gst", dma=True)
                mb4 = p.ctr["gst"]
                m_gt_used = 0
                castf = {}

                def fu_load(it):
                    m, gi = it // 2, it % 2
                    wq = (wq_g, wq_u)[gi]
                    sl = it % 2
                    isem = "i0" if sl == 0 else "i1"
                    p.wait("sync", "ve", m_cast[sl])
                    p.emit("sync", lambda e, m=m, sl=sl, wq=wq: e.dma_start(
                        out=istg[sl][:, 0:16, :], in_=wq[m]), inc=isem, dma=True)
                    mwl = p.ctr[isem]
                    p.wait("vector", isem, mwl)
                    p.wait("vector", "te", m_wused[sl])
                    p.emit("vector", lambda e, sl=sl: e.tensor_copy(out=wt[sl][:, 0:16, :], in_=istg[sl][:, 0:16, :]), inc="ve")
                    m_cast[sl] = p.ctr["ve"]
                    castf[it] = p.ctr["ve"]
                fu_load(0)
                fu_load(1)
                for m in range(44):
                    for gi in range(2):
                        it = 2 * m + gi
                        sv = (sgv, suv)[gi]
                        fn = (ACT.Silu, ACT.Identity)[gi]
                        sl = it % 2
                        gtp = (gtmp, gtmpb)[gi]
                        p.wait("tensor", "ve", max(castf[it], mxn2))
                        p.wait("tensor", "ac", m_evA[sl])
                        p.wait("tensor", "ve", m_evV[sl])
                        for k in range(16):
                            p.emit("tensor", lambda e, k=k, sl=sl: e.matmul(
                                out=pA[sl][:, :], lhsT=wt[sl][:, k, :], rhs=xn2[:, k, :],
                                start=(k == 0), stop=(k == 15)), inc="te" if k == 15 else None)
                        m_wused[sl] = p.ctr["te"]
                        if it + 2 < 88:
                            fu_load(it + 2)
                        p.wait("vector", "te", m_wused[sl])
                        p.wait("vector", "gst", mb4)
                        p.wait("vector", "ac", m_gt_used)
                        p.emit("vector", lambda e, sl=sl, gtp=gtp: e.tensor_tensor(
                            out=gtp[:, :], in0=pA[sl][:, :], in1=cs_rep[:, :], op=ALU.subtract), inc="ve")
                        m_evV[sl] = p.ctr["ve"]
                        mev = p.ctr["ve"]
                        p.wait("scalar", "ve", mev)
                        if gi == 0:
                            p.emit("scalar", lambda e, m=m, sv=sv, fn=fn, gtp=gtp: e.activation(
                                out=gt[:, :], in_=gtp[:, :], func=fn, scale=sv[:, m:m + 1]), inc="ac")
                        else:
                            p.emit("scalar", lambda e, m=m, sv=sv, fn=fn, gtp=gtp: e.activation(
                                out=ut[:, :], in_=gtp[:, :], func=fn, scale=sv[:, m:m + 1]), inc="ac")
                    mgu = p.ctr["ac"]
                    p.wait("vector", "ac", mgu)
                    p.emit("vector", lambda e, m=m: e.tensor_tensor(
                        out=area[:, m, :], in0=gt[:, :], in1=ut[:, :], op=ALU.mult), inc="ve")
                    m_gt_used = mgu
                mguall = p.ctr["ve"]
                p.wait("tensor", "ve", max(mguall, m_stats_used))
                for k in range(44):
                    p.emit("tensor", lambda e, k=k: e.matmul(
                        out=pst0[0:1, 0:TH], lhsT=onesb[:], rhs=area[:, k, :],
                        start=(k == 0), stop=(k == 43)), inc="te" if k == 43 else None)
                mcs2 = p.ctr["te"]
                p.wait("vector", "te", mcs2)
                p.emit("vector", lambda e: e.tensor_copy(out=pstat[0:1, 3, :], in_=pst0[0:1, 0:TH]), inc="ve")
                mcs2v = p.ctr["ve"]
                m_stats_used = mcs2v
                p.wait("gpsimd", "ve", mcs2v)
                p.emit("gpsimd", lambda e: e.dma_start(out=srow[1:2, :], in_=pstat[0:1, 3, :]), inc="gst", dma=True)
                mb5 = p.ctr["gst"]
                p.wait("gpsimd", "gst", mb5)
                p.emit("gpsimd", lambda e: e.dma_start(out=cs2_rep[:, :], in_=srow[1:2, :].partition_broadcast(128)), inc="gst", dma=True)
                mb6 = p.ctr["gst"]
                castd = {}

                def dn_load(m):
                    sl = m % 2
                    isem = "i0" if sl == 0 else "i1"
                    for gof, gn in ((0, 16), (16, 16), (32, 12)):
                        p.wait("sync", "ve", m_cast[sl])
                        p.emit("sync", lambda e, m=m, sl=sl, gof=gof, gn=gn: e.dma_start(
                            out=istg[sl][:, 0:gn, :], in_=wq_d[m, :, gof:gof + gn, :]), inc=isem, dma=True)
                        mwl = p.ctr[isem]
                        p.wait("vector", isem, mwl)
                        if gof == 0:
                            p.wait("vector", "te", m_wused[sl])
                        p.emit("vector", lambda e, sl=sl, gof=gof, gn=gn: e.tensor_copy(
                            out=wt[sl][:, gof:gof + gn, :], in_=istg[sl][:, 0:gn, :]), inc="ve")
                        m_cast[sl] = p.ctr["ve"]
                    castd[m] = p.ctr["ve"]
                dn_load(0)
                for m in range(16):
                    sl = m % 2
                    if m + 1 < 16:
                        dn_load(m + 1)
                    p.wait("sync", "ve", m_x1r_used)
                    p.wait("sync", "gx1", m_x1sp)
                    p.emit("sync", lambda e, m=m, th=th: e.dma_start(
                        out=x1r[:, :], in_=x1sp[th, :, m, :]), inc="xr", dma=True)
                    mx1l = p.ctr["xr"]
                    p.wait("tensor", "ve", castd[m])
                    p.wait("tensor", "ac", m_evA[sl])
                    p.wait("tensor", "ve", m_evV[sl])
                    for k in range(44):
                        p.emit("tensor", lambda e, k=k, sl=sl: e.matmul(
                            out=pA[sl][:, :], lhsT=wt[sl][:, k, :], rhs=area[:, k, :],
                            start=(k == 0), stop=(k == 43)), inc="te" if k == 43 else None)
                    m_wused[sl] = p.ctr["te"]
                    p.wait("vector", "te", m_wused[sl])
                    p.wait("vector", "gst", mb6)
                    p.wait("vector", "xr", mx1l)
                    p.wait("vector", "ve", msd)
                    p.wait("vector", "go", m_out_done)
                    p.emit("vector", lambda e, m=m: e.scalar_tensor_tensor(
                        out=gtmp[:, :], in0=cs2_rep[:, :], scalar=sd128[:, m:m + 1], in1=x1r[:, :],
                        op0=ALU.mult, op1=ALU.add))
                    p.emit("vector", lambda e, m=m, sl=sl: e.scalar_tensor_tensor(
                        out=otile[:, :], in0=pA[sl][:, :], scalar=sdv[:, m:m + 1], in1=gtmp[:, :],
                        op0=ALU.mult, op1=ALU.add), inc="ve")
                    m_evV[sl] = p.ctr["ve"]
                    m_x1r_used = p.ctr["ve"]
                    mot = p.ctr["ve"]
                    p.wait("gpsimd", "ve", mot)
                    p.emit("gpsimd", lambda e, m=m, t0=t0: e.dma_start(
                        out=o_T[:, m, t0:t0 + TH], in_=otile[:, :]), inc="go", dma=True)
                    m_out_done = p.ctr["go"]
            for nmm in ("go", "gx1", "gst"):
                p.wait("gpsimd", nmm, p.ctr[nmm])
                p.wait("sync", nmm, p.ctr[nmm])
            p.replay()
    return nc



def _chunk(a, p=128):
    n, t = a.shape
    c = n // p
    return np.ascontiguousarray(a.reshape(c, p, t).transpose(1, 0, 2))


def _unchunk(a):
    p, c, t = a.shape
    return np.ascontiguousarray(a.transpose(1, 0, 2).reshape(c * p, t))


def _vec_chunk(v, p=128):
    return np.ascontiguousarray(v.reshape(-1, p).T)


def kernel(L0_attn_hidden_states, L0_attn_in_proj_w, L0_attn_conv1d_w,
           L0_attn_x_proj_w, L0_attn_dt_proj_w, L0_attn_dt_proj_bias,
           L0_attn_A_log, L0_attn_D_param, L0_attn_out_proj_w,
           L0_ffn_wq_gate, L0_ffn_s_gate, L0_ffn_wq_up, L0_ffn_s_up,
           L0_ffn_wq_down, L0_ffn_s_down):
    X = np.asarray(L0_attn_hidden_states)[0]
    XT = np.ascontiguousarray(X.T)

    w_inx0 = _chunk(np.ascontiguousarray(np.asarray(L0_attn_in_proj_w)[:DIN].T))
    w_inx = np.ascontiguousarray(w_inx0.reshape(128, 16, 16, 256).transpose(2, 0, 1, 3))
    convw = _chunk(np.asarray(L0_attn_conv1d_w))
    w_xp = _chunk(np.ascontiguousarray(np.asarray(L0_attn_x_proj_w).T))
    w_dt = np.ascontiguousarray(np.asarray(L0_attn_dt_proj_w).T)
    dt_b = _vec_chunk(np.asarray(L0_attn_dt_proj_bias))
    alog = _chunk(np.asarray(L0_attn_A_log).reshape(DIN, NST))
    w_inz0 = _chunk(np.ascontiguousarray(np.asarray(L0_attn_in_proj_w)[DIN:].T))
    w_inz = np.ascontiguousarray(w_inz0.reshape(128, 16, 32, 128).transpose(2, 0, 1, 3))
    w_out0 = _chunk(np.ascontiguousarray(np.asarray(L0_attn_out_proj_w).T))
    w_out = np.ascontiguousarray(w_out0.reshape(128, 32, 16, 128).transpose(2, 0, 1, 3))
    wq_g0 = _chunk(np.ascontiguousarray(np.asarray(L0_ffn_wq_gate).T))
    wq_g = np.ascontiguousarray(wq_g0.reshape(128, 16, 44, 128).transpose(2, 0, 1, 3))
    wq_u0 = _chunk(np.ascontiguousarray(np.asarray(L0_ffn_wq_up).T))
    wq_u = np.ascontiguousarray(wq_u0.reshape(128, 16, 44, 128).transpose(2, 0, 1, 3))
    wq_d0 = _chunk(np.ascontiguousarray(np.asarray(L0_ffn_wq_down).T))
    wq_d = np.ascontiguousarray(wq_d0.reshape(128, 44, 16, 128).transpose(2, 0, 1, 3))
    s_g = _vec_chunk(np.asarray(L0_ffn_s_gate))
    s_u = _vec_chunk(np.asarray(L0_ffn_s_up))
    s_d = _vec_chunk(np.asarray(L0_ffn_s_down))
    dpar = _vec_chunk(np.asarray(L0_attn_D_param))

    nc = build_fused()
    ins = []
    for c in range(NC):
        pad = (NC - 1 - c) * TC
        fr = np.zeros((D, SEQ), np.float32)
        fr[:, pad:] = XT[:, :SEQ - pad] if pad else XT
        ins.append({
            "xTf": _chunk(fr),
            "w_inx": w_inx, "convw": convw, "w_xp": w_xp,
            "w_dt": w_dt, "dt_b": dt_b, "alog_i": alog,
            "w_inz": w_inz, "w_out": w_out,
            "wq_g": wq_g, "wq_u": wq_u, "wq_d": wq_d,
            "s_g": s_g, "s_u": s_u, "s_d": s_d, "dpar": dpar,
        })
    r = run_bass_kernel_spmd(nc, ins, core_ids=list(range(NC)))
    oT = np.concatenate([_unchunk(r.results[c]["o_T"]) for c in range(NC)], axis=1)
    return np.ascontiguousarray(oT.T).reshape(1, SEQ, D).astype(np.float32)


# version 10
# speedup vs baseline: 1.4828x; 1.4828x over previous
import contextlib
import numpy as np
import ml_dtypes
import concourse.bass as bass
import concourse.mybir as mybir
from concourse.bass_utils import run_bass_kernel_spmd

F32 = mybir.dt.float32
BF16 = mybir.dt.bfloat16
I32 = mybir.dt.int32
ALU = mybir.AluOpType
ACT = mybir.ActivationFunctionType

D = 2048
SEQ = 8192
DIN = 4096
NST = 16
KCV = 4
HID = 5632
NC = 8
TC = SEQ // NC
TB = 512
EPS = 1e-5
BF = ml_dtypes.bfloat16

ENGS = ("sync", "vector", "scalar", "tensor", "gpsimd")


class Prog:
    def __init__(self, nc, stack, extra=()):
        self.nc = nc
        self.stack = stack
        self.ev = {k: [] for k in ENGS}
        self.sem = {}
        self.ctr = {}
        for name in ("te", "ve", "ac") + tuple(extra):
            self.sem[name] = stack.enter_context(nc.semaphore("s_" + name))
            self.ctr[name] = 0

    def emit(self, eng, fn, inc=None, dma=False):
        amt = 16 if dma else 1
        if inc is not None:
            self.ctr[inc] += amt
        sem = self.sem[inc] if inc is not None else None

        def run(e, fn=fn, sem=sem, amt=amt):
            inst = fn(e)
            if sem is not None:
                inst.then_inc(sem, amt)
        self.ev[eng].append(run)
        return self.ctr[inc] if inc is not None else None

    AUDIT = False

    def wait(self, eng, name, val):
        if val <= 0:
            return
        if Prog.AUDIT and name not in ("te", "ve", "ac", "gu") and val != self.ctr[name]:
            print(f"NONCUM wait: eng={eng} sem={name} val={val} cur={self.ctr[name]}")
        sem = self.sem[name]
        self.ev[eng].append(lambda e, s=sem, v=val: e.wait_ge(s, v))

    def barrier(self):
        snap = dict(self.ctr)
        for eng in ENGS:
            for s, v in snap.items():
                self.wait(eng, s, v)

    def replay(self):
        with self.nc.Block() as blk:
            blk.sync(lambda e: [f(e) for f in self.ev["sync"]])
            blk.vector(lambda e: [f(e) for f in self.ev["vector"]])
            blk.scalar(lambda e: [f(e) for f in self.ev["scalar"]])
            blk.tensor(lambda e: [f(e) for f in self.ev["tensor"]])
            blk.gpsimd(lambda e: [f(e) for f in self.ev["gpsimd"]])


DEBUG_SPILLS = False
RACE_DETECT = False

def build_fused():
    W = 4096
    POFF = SEQ - TC
    nc = bass.Bass("TRN2", target_bir_lowering=False, debug=False,
                   detect_race_conditions=RACE_DETECT)
    xTf = nc.dram_tensor("xTf", [128, 16, SEQ], F32, kind="ExternalInput").ap()
    w_inx = nc.dram_tensor("w_inx", [16, 128, 16, 256], F32, kind="ExternalInput").ap()
    convw = nc.dram_tensor("convw", [128, 32, KCV], F32, kind="ExternalInput").ap()
    w_xp = nc.dram_tensor("w_xp", [128, 32, 160], F32, kind="ExternalInput").ap()
    w_dt = nc.dram_tensor("w_dt", [128, DIN], F32, kind="ExternalInput").ap()
    dt_b = nc.dram_tensor("dt_b", [128, 32], F32, kind="ExternalInput").ap()
    alog_i = nc.dram_tensor("alog_i", [128, 32, NST], F32, kind="ExternalInput").ap()
    w_inz = nc.dram_tensor("w_inz", [32, 128, 16, 128], F32, kind="ExternalInput").ap()
    w_out = nc.dram_tensor("w_out", [16, 128, 32, 128], F32, kind="ExternalInput").ap()
    wq_g = nc.dram_tensor("wq_g", [44, 128, 16, 128], I32, kind="ExternalInput").ap()
    wq_u = nc.dram_tensor("wq_u", [44, 128, 16, 128], I32, kind="ExternalInput").ap()
    wq_d = nc.dram_tensor("wq_d", [16, 128, 44, 128], I32, kind="ExternalInput").ap()
    s_g = nc.dram_tensor("s_g", [128, 44], F32, kind="ExternalInput").ap()
    s_u = nc.dram_tensor("s_u", [128, 44], F32, kind="ExternalInput").ap()
    s_d = nc.dram_tensor("s_d", [128, 16], F32, kind="ExternalInput").ap()
    dpar = nc.dram_tensor("dpar", [128, 32], F32, kind="ExternalInput").ap()
    o_T = nc.dram_tensor("o_T", [128, 16, TC], F32, kind="ExternalOutput").ap()

    KI = "ExternalOutput" if DEBUG_SPILLS else "Internal"
    wq_img = nc.dram_tensor("wq_img", [4, 128, 16, 1024], BF16, kind="Internal").ap()
    xn_sp = nc.dram_tensor("xn_sp", [128, 16, SEQ], BF16, kind=KI).ap()
    xc_sp = nc.dram_tensor("xc_sp", [128, 32, SEQ], BF16, kind=KI).ap()
    d_sp = nc.dram_tensor("d_sp", [128, 32, SEQ], F32, kind=KI).ap()
    u_sp = nc.dram_tensor("u_sp", [128, 32, SEQ], BF16, kind=KI).ap()
    b_sp = nc.dram_tensor("b_sp", [16, SEQ], F32, kind=KI).ap()
    c_sp = nc.dram_tensor("c_sp", [16, SEQ], F32, kind=KI).ap()
    bimg = nc.dram_tensor("bimg", [16, SEQ], BF16, kind="Internal").ap()
    cimg = nc.dram_tensor("cimg", [16, SEQ], BF16, kind="Internal").ap()
    y_sp = nc.dram_tensor("y_sp", [128, 32, TC], F32, kind=KI).ap()
    x1sp = nc.dram_tensor("x1sp", [2, 128, 16, TB], F32, kind="Internal").ap()
    srow = nc.dram_tensor("srow", [2, TB], F32, kind="Internal").ap()

    stack = contextlib.ExitStack()
    with stack:
        psf = lambda n, s: stack.enter_context(nc.psum_tensor(n, s, F32))
        pA = [psf("pA0", [128, TB]), psf("pA1", [128, TB])]
        pst0 = psf("pst0", [1, TB])
        pst1 = psf("pst1", [1, TB])
        px2 = [psf("px0", [128, TB]), psf("px1", [128, TB])]
        onesb = stack.enter_context(nc.sbuf_tensor("onesb", [128, 1], BF16))
        onesf = stack.enter_context(nc.sbuf_tensor("onesf", [128, 1], F32))

        p = Prog(nc, stack, extra=(
            "ws", "sw", "xt", "gst", "gxn", "xa", "gxc", "xc0", "xc1",
            "gout", "goutB", "gw", "wsB", "gbc", "gi", "d1", "u1", "r0", "r1", "gy", "gy1", "gu",
            "f0", "f1", "i0", "i1", "xr", "yx", "gx1", "go"))

        p.emit("vector", lambda e: e.memset(onesb[:], 1.0), inc="ve")
        p.emit("vector", lambda e: e.memset(onesf[:], 1.0), inc="ve")

        with contextlib.ExitStack() as ph:
            sb = lambda n, s, d: ph.enter_context(nc.sbuf_tensor(n, s, d))
            wq4 = sb("wq4", [128, 16, 1024], BF16)
            fscr = sb("fscr", [128, 5120], F32)
            xpart = sb("xpart", [128, 16, TB + 8], F32)
            xnb = sb("xnb", [128, 16, TB], BF16)
            xcb = sb("xcb", [128, 8, TB], BF16)
            xcld = sb("xcld", [128, 2, TB], BF16)
            xcv2 = [sb("xcv0", [128, TB], F32), sb("xcv1", [128, TB], F32)]
            xdbl = sb("xdbl", [128, 2, TC], F32)
            dtr = sb("dtr", [128, TC], BF16)
            wxp = sb("wxp", [128, 32, 160], BF16)
            wdt = sb("wdt", [128, DIN], BF16)
            cw = sb("cw", [128, 32, KCV], F32)
            dtb = sb("dtb", [128, 32], F32)
            stat = sb("stat", [1, 6, TB], F32)
            mrep = sb("mrep", [128, TB], F32)
            rrep = sb("rrep", [128, TB], F32)
            dsc0 = [sb("dsc0a", [128, TB], F32), sb("dsc0b", [128, TB], F32)]
            dsc1 = [sb("dsc1a", [128, TB], F32), sb("dsc1b", [128, TB], F32)]
            dblk = [sb("dblka", [128, TB], F32), sb("dblkb", [128, TB], F32)]
            ublk = [sb("ublka", [128, TB], BF16), sb("ublkb", [128, TB], BF16)]
            xhalo = sb("xhalo", [128, 32, 3], F32)
            xh2 = sb("xh2", [128, 8, 3], F32)

            p.emit("vector", lambda e: e.memset(xhalo[:], 0.0), inc="ve")
            p.emit("sync", lambda e: e.dma_start(
                out=fscr[:, 0:5120].rearrange("p (m e) -> p m e", m=32), in_=w_xp), inc="ws", dma=True)
            mw0 = p.ctr["ws"]
            p.wait("vector", "ws", mw0)
            p.emit("vector", lambda e: e.tensor_copy(
                out=wxp[:], in_=fscr[:, 0:5120].rearrange("p (m e) -> p m e", m=32)), inc="ve")
            p.wait("sync", "ve", p.ctr["ve"])
            p.emit("sync", lambda e: e.dma_start(out=fscr[:, 0:4096], in_=w_dt), inc="ws", dma=True)
            mw1 = p.ctr["ws"]
            p.wait("vector", "ws", mw1)
            p.emit("vector", lambda e: e.tensor_copy(out=wdt[:], in_=fscr[:, 0:4096]), inc="ve")
            mwdt = p.ctr["ve"]
            p.emit("sync", lambda e: e.dma_start(out=cw[:], in_=convw), inc="sw", dma=True)
            p.emit("sync", lambda e: e.dma_start(out=dtb[:], in_=dt_b), inc="sw", dma=True)
            msw = p.ctr["sw"]
            p.wait("vector", "sw", msw)
            p.wait("scalar", "sw", msw)

            m_wcast = mwdt
            m_fscr_te = 0
            m_wq = ("ve", 0)
            m_wq_h = [("ws", 0), ("ws", 0)]
            m_half_free = [0, 0]
            m_wq_spill = 0
            m_xn_used = 0
            m_xn_hbm = 0
            m_conv_done = 0
            m_norm_done = 0
            m_stats_used = 0
            evac_marks = [0] * 8
            sil_marks = [0, 0]
            m_xc_hbm = 0
            m_xproj_te = 0
            m_xdbl_ve = 0
            m_px_ve = [0, 0]
            m_px_ac = [0, 0]
            m_dsc_free = [0, 0]
            m_u_slot = [0, 0]
            m_store_slot = [0, 0]
            m_dt_te = 0
            m_bc = 0

            for b in range(8):
                base = b * TC
                for (c0, w) in ((0, TB), (TB, TB)):
                    p.wait("sync", "ve", max(m_norm_done, m_conv_done))
                    p.emit("sync", lambda e, c0=c0, w=w, base=base: e.dma_start(
                        out=xpart[:, 0:8, 0:w], in_=xTf[:, 0:8, base + c0:base + c0 + w]), inc="xt", dma=True)
                    p.emit("sync", lambda e, c0=c0, w=w, base=base: e.dma_start(
                        out=xpart[:, 8:16, 0:w], in_=xTf[:, 8:16, base + c0:base + c0 + w]), inc="xt", dma=True)
                    mldx = p.ctr["xt"]
                    p.wait("tensor", "xt", mldx)
                    p.wait("tensor", "ve", m_stats_used)
                    for k in range(16):
                        p.emit("tensor", lambda e, k=k, w=w: e.matmul(
                            out=pst0[0:1, 0:w], lhsT=onesf[:], rhs=xpart[:, k, 0:w],
                            start=(k == 0), stop=(k == 15)),
                            inc="te" if (k == 15) else None)
                    fv = fscr[:, 0:4096].rearrange("p (a c) -> p a c", a=16)
                    for sp2 in range(2):
                        c2 = 256 * sp2
                        p.wait("scalar", "xt", mldx)
                        p.wait("scalar", "ve", m_wcast)
                        p.wait("scalar", "te", m_fscr_te)
                        p.emit("scalar", lambda e, c2=c2, fv=fv, w=w: e.activation(
                            out=fv, in_=xpart[:, :, c2:c2 + 256], func=ACT.Square), inc="ac")
                        msq = p.ctr["ac"]
                        p.wait("tensor", "ac", msq)
                        for k in range(16):
                            p.emit("tensor", lambda e, k=k, c2=c2, fv=fv: e.matmul(
                                out=pst1[0:1, c2:c2 + 256], lhsT=onesf[:], rhs=fv[:, k, :],
                                start=(k == 0), stop=(k == 15)),
                                inc="te" if (k == 15) else None)
                        m_fscr_te = p.ctr["te"]
                    mst = p.ctr["te"]
                    p.wait("vector", "te", mst)
                    p.emit("vector", lambda e, w=w: e.tensor_scalar_mul(out=stat[0:1, 0, 0:w], in0=pst0[0:1, 0:w], scalar1=1.0 / D))
                    p.emit("vector", lambda e, w=w: e.tensor_scalar_mul(out=stat[0:1, 1, 0:w], in0=pst1[0:1, 0:w], scalar1=1.0 / D))
                    p.emit("vector", lambda e, w=w: e.tensor_tensor(out=stat[0:1, 2, 0:w], in0=stat[0:1, 0, 0:w], in1=stat[0:1, 0, 0:w], op=ALU.mult))
                    p.emit("vector", lambda e, w=w: e.tensor_tensor(out=stat[0:1, 3, 0:w], in0=stat[0:1, 1, 0:w], in1=stat[0:1, 2, 0:w], op=ALU.subtract))
                    p.emit("vector", lambda e, w=w: e.tensor_scalar_add(out=stat[0:1, 3, 0:w], in0=stat[0:1, 3, 0:w], scalar1=EPS), inc="ve")
                    mvar = p.ctr["ve"]
                    m_stats_used = mvar
                    p.wait("scalar", "ve", mvar)
                    p.emit("scalar", lambda e, w=w: e.activation(out=stat[0:1, 4, 0:w], in_=stat[0:1, 3, 0:w], func=ACT.Sqrt), inc="ac")
                    msqr = p.ctr["ac"]
                    p.wait("vector", "ac", msqr)
                    p.emit("vector", lambda e, w=w: e.reciprocal(out=stat[0:1, 5, 0:w], in_=stat[0:1, 4, 0:w]), inc="ve")
                    mrs = p.ctr["ve"]
                    p.wait("gpsimd", "ve", mrs)
                    p.emit("gpsimd", lambda e, w=w: e.dma_start(out=srow[0:1, 0:w], in_=stat[0:1, 0, 0:w]), inc="gst", dma=True)
                    p.emit("gpsimd", lambda e, w=w: e.dma_start(out=srow[1:2, 0:w], in_=stat[0:1, 5, 0:w]), inc="gst", dma=True)
                    mb1 = p.ctr["gst"]
                    p.wait("gpsimd", "gst", mb1)
                    p.emit("gpsimd", lambda e, w=w: e.dma_start(out=mrep[:, 0:w], in_=srow[0:1, 0:w].partition_broadcast(128)), inc="gst", dma=True)
                    p.emit("gpsimd", lambda e, w=w: e.dma_start(out=rrep[:, 0:w], in_=srow[1:2, 0:w].partition_broadcast(128)), inc="gst", dma=True)
                    mb2 = p.ctr["gst"]
                    p.wait("vector", "gst", mb2)
                    p.wait("vector", "te", m_xn_used)
                    for k in range(16):
                        p.emit("vector", lambda e, k=k, w=w: e.tensor_tensor(
                            out=xpart[:, k, 0:w], in0=xpart[:, k, 0:w], in1=mrep[:, 0:w], op=ALU.subtract))
                        p.emit("vector", lambda e, k=k, w=w: e.tensor_tensor(
                            out=xnb[:, k, 0:w], in0=xpart[:, k, 0:w], in1=rrep[:, 0:w], op=ALU.mult))
                    p.emit("vector", lambda e: e.engine_nop(), inc="ve")
                    m_norm_done = p.ctr["ve"]
                    p.wait("gpsimd", "ve", m_norm_done)
                    p.emit("gpsimd", lambda e, c0=c0, w=w, base=base: e.dma_start(
                        out=xn_sp[:, 0:8, base + c0:base + c0 + w], in_=xnb[:, 0:8, 0:w]), inc="gxn", dma=True)
                    p.emit("gpsimd", lambda e, c0=c0, w=w, base=base: e.dma_start(
                        out=xn_sp[:, 8:16, base + c0:base + c0 + w], in_=xnb[:, 8:16, 0:w]), inc="gxn", dma=True)
                    m_xn_hbm = p.ctr["gxn"]

                for q in range(4):
                    if b == 0:
                        for g in range(4):
                            gj = 4 * q + g
                            p.wait("sync", "ve", m_wcast)
                            p.wait("sync", "te", m_fscr_te)
                            p.emit("sync", lambda e, gj=gj: e.dma_start(
                                out=fscr[:, 0:4096].rearrange("p (k c) -> p k c", k=16), in_=w_inx[gj]), inc="ws", dma=True)
                            mw = p.ctr["ws"]
                            p.wait("vector", "ws", mw)
                            if g == 0:
                                p.wait("vector", "te", m_xn_used)
                                p.wait("vector", "gw", m_wq_spill)
                            p.emit("vector", lambda e, g=g: e.tensor_copy(
                                out=wq4[:, :, 256 * g:256 * (g + 1)], in_=fscr[:, 0:4096].rearrange("p (k c) -> p k c", k=16)), inc="ve")
                            m_wcast = p.ctr["ve"]
                        m_wq = ("ve", m_wcast)
                        p.wait("gpsimd", "ve", m_wcast)
                        p.emit("gpsimd", lambda e, q=q: e.dma_start(
                            out=wq_img[q, :, 0:8, :], in_=wq4[:, 0:8, :]), inc="gw", dma=True)
                        p.emit("gpsimd", lambda e, q=q: e.dma_start(
                            out=wq_img[q, :, 8:16, :], in_=wq4[:, 8:16, :]), inc="gw", dma=True)
                        m_wq_spill = p.ctr["gw"]
                    else:
                        p.wait("sync", "gw", m_wq_spill)
                        p.wait("sync", "te", m_half_free[0])
                        p.emit("sync", lambda e, q=q: e.dma_start(
                            out=wq4[:, 0:8, 0:512], in_=wq_img[q, :, 0:8, 0:512]), inc="ws", dma=True)
                        p.emit("sync", lambda e, q=q: e.dma_start(
                            out=wq4[:, 8:16, 0:512], in_=wq_img[q, :, 8:16, 0:512]), inc="ws", dma=True)
                        mh0 = p.ctr["ws"]
                        p.wait("sync", "te", m_half_free[1])
                        p.emit("sync", lambda e, q=q: e.dma_start(
                            out=wq4[:, 0:8, 512:1024], in_=wq_img[q, :, 0:8, 512:1024]), inc="wsB", dma=True)
                        p.emit("sync", lambda e, q=q: e.dma_start(
                            out=wq4[:, 8:16, 512:1024], in_=wq_img[q, :, 8:16, 512:1024]), inc="wsB", dma=True)
                        m_wq_h = [("ws", mh0), ("wsB", p.ctr["wsB"])]
                    for b2 in range(2):
                        p.wait("sync", "gxn", m_xn_hbm)
                        p.wait("sync", "te", m_xn_used)
                        p.emit("sync", lambda e, b2=b2, base=base: e.dma_start(
                            out=xnb[:, 0:8, :], in_=xn_sp[:, 0:8, base + TB * b2:base + TB * b2 + TB]), inc="xa", dma=True)
                        p.emit("sync", lambda e, b2=b2, base=base: e.dma_start(
                            out=xnb[:, 8:16, :], in_=xn_sp[:, 8:16, base + TB * b2:base + TB * b2 + TB]), inc="xa", dma=True)
                        mxn = p.ctr["xa"]
                        if b == 0:
                            p.wait("tensor", m_wq[0], m_wq[1])
                        p.wait("tensor", "xa", mxn)
                        p.wait("tensor", "ve", m_conv_done)
                        for m in range(8):
                            sl = m % 2
                            if b > 0 and m % 4 == 0:
                                mwh = m_wq_h[m // 4]
                                p.wait("tensor", mwh[0], mwh[1])
                            if m >= 2:
                                p.wait("tensor", "ac", evac_marks[m - 2])
                            for k in range(16):
                                p.emit("tensor", lambda e, m=m, k=k, sl=sl: e.matmul(
                                    out=pA[sl][:, 0:TB], lhsT=wq4[:, k, 128 * m:128 * (m + 1)],
                                    rhs=xnb[:, k, :], start=(k == 0), stop=(k == 15)),
                                    inc="te" if k == 15 else None)
                            mte = p.ctr["te"]
                            if b2 == 1 and (m == 3 or m == 7):
                                m_half_free[m // 4] = mte
                            p.wait("scalar", "te", mte)
                            p.emit("scalar", lambda e, m=m, sl=sl: e.activation(
                                out=xpart[:, m, 3:TB + 3], in_=pA[sl][:, 0:TB], func=ACT.Identity), inc="ac")
                            evac_marks[m] = p.ctr["ac"]
                        m_xn_used = p.ctr["te"]
                        m_evac = p.ctr["ac"]
                        p.wait("vector", "ac", m_evac)
                        p.wait("vector", "te", m_xproj_te)
                        p.wait("vector", "gxc", m_xc_hbm)
                        for m in range(8):
                            mc = 8 * q + m
                            xv = xcv2[m % 2]
                            hsrc = (xhalo[:, mc, :] if b2 == 0 else xh2[:, m, :])
                            p.emit("vector", lambda e, m=m, hsrc=hsrc: e.tensor_copy(
                                out=xpart[:, m, 0:3], in_=hsrc))
                            p.wait("vector", "ac", sil_marks[m % 2])
                            p.emit("vector", lambda e, mc=mc, m=m, xv=xv: e.tensor_scalar_mul(
                                out=xv[:, :], in0=xpart[:, m, 0:TB], scalar1=cw[:, mc, 0:1]))
                            for k in range(1, KCV):
                                p.emit("vector", lambda e, m=m, mc=mc, k=k, xv=xv: e.scalar_tensor_tensor(
                                    out=xv[:, :], in0=xpart[:, m, k:TB + k], scalar=cw[:, mc, k:k + 1],
                                    in1=xv[:, :], op0=ALU.mult, op1=ALU.add), inc="ve" if k == KCV - 1 else None)
                            mcv = p.ctr["ve"]
                            hdst = (xh2[:, m, :] if b2 == 0 else xhalo[:, mc, :])
                            p.emit("vector", lambda e, m=m, hdst=hdst: e.tensor_copy(
                                out=hdst, in_=xpart[:, m, TB:TB + 3]))
                            p.wait("scalar", "ve", mcv)
                            p.emit("scalar", lambda e, m=m, xv=xv: e.activation(
                                out=xcb[:, m, :], in_=xv[:, :], func=ACT.Silu), inc="ac")
                            sil_marks[m % 2] = p.ctr["ac"]
                        p.emit("vector", lambda e: e.engine_nop(), inc="ve")
                        m_conv_done = p.ctr["ve"]
                        msil = p.ctr["ac"]
                        p.wait("gpsimd", "ac", msil)
                        p.emit("gpsimd", lambda e, q=q, b2=b2, base=base: e.dma_start(
                            out=xc_sp[:, 8 * q:8 * (q + 1), base + TB * b2:base + TB * (b2 + 1)], in_=xcb[:, :, :]),
                            inc="gxc", dma=True)
                        m_xc_hbm = p.ctr["gxc"]
                        p.wait("tensor", "ac", msil)
                        for mo, mwd in ((0, 128), (128, 32)):
                            pxs = 0 if mo == 0 else 1
                            p.wait("tensor", "ve", m_px_ve[pxs])
                            p.wait("tensor", "ac", m_px_ac[pxs])
                            for k in range(8):
                                p.emit("tensor", lambda e, mo=mo, mwd=mwd, k=k, q=q, pxs=pxs: e.matmul(
                                    out=px2[pxs][0:mwd, 0:TB], lhsT=wxp[:, 8 * q + k, mo:mo + mwd],
                                    rhs=xcb[:, k, :], start=(k == 0), stop=(k == 7)),
                                    inc="te" if k == 7 else None)
                            mxp = p.ctr["te"]
                            p.wait("vector", "te", mxp)
                            ch = 0 if mo == 0 else 1
                            if q == 0:
                                p.wait("vector", "gbc", m_bc)
                                p.emit("vector", lambda e, ch=ch, mwd=mwd, b2=b2, pxs=pxs: e.tensor_copy(
                                    out=xdbl[0:mwd, ch, TB * b2:TB * (b2 + 1)], in_=px2[pxs][0:mwd, 0:TB]), inc="ve")
                            else:
                                p.emit("vector", lambda e, ch=ch, mwd=mwd, b2=b2, pxs=pxs: e.tensor_tensor(
                                    out=xdbl[0:mwd, ch, TB * b2:TB * (b2 + 1)], in0=xdbl[0:mwd, ch, TB * b2:TB * (b2 + 1)],
                                    in1=px2[pxs][0:mwd, 0:TB], op=ALU.add), inc="ve")
                            m_xdbl_ve = p.ctr["ve"]
                            m_px_ve[pxs] = p.ctr["ve"]
                        m_xproj_te = p.ctr["te"]

                p.wait("vector", "ve", m_xdbl_ve)
                p.wait("vector", "te", m_dt_te)
                p.emit("vector", lambda e: e.tensor_copy(out=dtr[:], in_=xdbl[:, 0, :]), inc="ve")
                mdtr = p.ctr["ve"]
                p.wait("gpsimd", "ve", mdtr)
                p.emit("gpsimd", lambda e, base=base: e.dma_start(
                    out=b_sp[:, base:base + TC], in_=xdbl[0:16, 1, :]), inc="gbc", dma=True)
                p.emit("gpsimd", lambda e, base=base: e.dma_start(
                    out=c_sp[:, base:base + TC], in_=xdbl[16:32, 1, :]), inc="gbc", dma=True)
                m_bc = p.ctr["gbc"]

                for m in range(32):
                    for b2 in range(2):
                        par = (m * 2 + b2) % 2
                        xcsem = "xc0" if par == 0 else "xc1"
                        p.wait("sync", "gxc", m_xc_hbm)
                        p.wait("sync", "ve", m_u_slot[par])
                        p.emit("sync", lambda e, m=m, b2=b2, par=par, base=base: e.dma_start(
                            out=xcld[:, par, :], in_=xc_sp[:, m, base + TB * b2:base + TB * (b2 + 1)]), inc=xcsem, dma=True)
                        mxc = p.ctr[xcsem]
                        p.wait("tensor", "ve", mdtr)
                        p.wait("tensor", "ac", m_px_ac[par])
                        p.wait("tensor", "ve", m_px_ve[par])
                        p.emit("tensor", lambda e, m=m, b2=b2, par=par: e.matmul(
                            out=px2[par][:, 0:TB], lhsT=wdt[:, 128 * m:128 * (m + 1)],
                            rhs=dtr[:, TB * b2:TB * (b2 + 1)], start=True, stop=True), inc="te")
                        mdt = p.ctr["te"]
                        p.wait("scalar", "te", mdt)
                        p.wait("scalar", "ve", m_dsc_free[par])
                        p.emit("scalar", lambda e, m=m, par=par: e.activation(
                            out=dsc0[par][:, :], in_=px2[par][:, 0:TB], func=ACT.Abs, bias=dtb[:, m:m + 1]))
                        p.emit("scalar", lambda e, m=m, par=par: e.activation(
                            out=dsc1[par][:, :], in_=px2[par][:, 0:TB], func=ACT.Relu, bias=dtb[:, m:m + 1]), inc="ac")
                        m_px_ac[par] = p.ctr["ac"]
                        p.emit("scalar", lambda e, par=par: e.activation(
                            out=dsc0[par][:, :], in_=dsc0[par][:, :], func=ACT.Exp, scale=-1.0))
                        p.emit("scalar", lambda e, par=par: e.activation(
                            out=dsc0[par][:, :], in_=dsc0[par][:, :], func=ACT.Ln, bias=1.0), inc="ac")
                        mln = p.ctr["ac"]
                        p.wait("vector", "ac", mln)
                        p.wait("vector", ("gout", "goutB")[par], m_store_slot[par])
                        p.emit("vector", lambda e, par=par: e.tensor_tensor(
                            out=dblk[par][:, :], in0=dsc0[par][:, :], in1=dsc1[par][:, :], op=ALU.add), inc="ve")
                        p.wait("vector", xcsem, mxc)
                        p.emit("vector", lambda e, par=par: e.tensor_tensor(
                            out=ublk[par][:, :], in0=dblk[par][:, :], in1=xcld[:, par, :], op=ALU.mult), inc="ve")
                        m_u_slot[par] = p.ctr["ve"]
                        m_dsc_free[par] = p.ctr["ve"]
                        gsem = ("gout", "goutB")[par]
                        p.wait("gpsimd", "ve", m_u_slot[par])
                        p.emit("gpsimd", lambda e, m=m, b2=b2, par=par, base=base, gsem=gsem: e.dma_start(
                            out=d_sp[:, m, base + TB * b2:base + TB * (b2 + 1)], in_=dblk[par][:, :]), inc=gsem, dma=True)
                        p.emit("gpsimd", lambda e, m=m, b2=b2, par=par, base=base, gsem=gsem: e.dma_start(
                            out=u_sp[:, m, base + TB * b2:base + TB * (b2 + 1)], in_=ublk[par][:, :]), inc=gsem, dma=True)
                        m_store_slot[par] = p.ctr[gsem]
                m_dt_te = p.ctr["te"]

            p.barrier()

        with contextlib.ExitStack() as ph:
            sb = lambda n, s, d: ph.enter_context(nc.sbuf_tensor(n, s, d))
            dl = [[sb("dl00", [128, W], F32), sb("dl01", [128, W], F32)],
                  [sb("dl10", [128, W], F32), sb("dl11", [128, W], F32)]]
            uu = [sb("uu0", [128, W], BF16), sb("uu1", [128, W], BF16)]
            dA2 = [sb("dA0", [128, W], F32), sb("dA1", [128, W], F32)]
            uB2 = [sb("uB0", [128, W], BF16), sb("uB1", [128, W], BF16)]
            h1b = sb("h1b", [128, W], BF16)
            brep = [sb("brep0", [128, W], BF16), sb("brep1", [128, W], BF16)]
            crep = [sb("crep0", [128, TC], BF16), sb("crep1", [128, TC], BF16)]
            trees = []
            for mi in range(2):
                trees.append({
                    "hc0": sb(f"hc0_{mi}", [128, TC], BF16),
                    "hc1": sb(f"hc1_{mi}", [128, TC], BF16),
                    "s2b": sb(f"s2b_{mi}", [128, TC], BF16),
                    "s4b": sb(f"s4b_{mi}", [128, TC], BF16),
                    "s8b": sb(f"s8b_{mi}", [128, TC], BF16),
                    "yb": sb(f"yb_{mi}", [128, TC], F32),
                })
            bcs = sb("bcs", [16, 1024], F32)
            bcb = sb("bcb", [16, 1024], BF16)
            aval = sb("aval", [128, 32, NST], F32)
            state = sb("state", [128, 32, NST], F32)

            p.emit("sync", lambda e: e.dma_start(out=aval[:], in_=alog_i), inc="sw", dma=True)
            ma = p.ctr["sw"]
            p.wait("scalar", "sw", ma)
            p.emit("scalar", lambda e: e.activation(out=aval[:], in_=aval[:], func=ACT.Exp), inc="ac")
            mae = p.ctr["ac"]
            p.wait("vector", "ac", mae)
            p.emit("vector", lambda e: e.tensor_scalar_mul(out=aval[:], in0=aval[:], scalar1=-1.0), inc="ve")
            mave = p.ctr["ve"]
            for src, dst in ((b_sp, bimg), (c_sp, cimg)):
                for hh in range(8):
                    p.wait("sync", "gi", p.ctr["gi"])
                    p.emit("sync", lambda e, src=src, hh=hh: e.dma_start(
                        out=bcs[:], in_=src[:, 1024 * hh:1024 * (hh + 1)]), inc="sw", dma=True)
                    p.wait("vector", "sw", p.ctr["sw"])
                    p.emit("vector", lambda e: e.tensor_copy(out=bcb[:], in_=bcs[:]), inc="ve")
                    p.wait("gpsimd", "ve", p.ctr["ve"])
                    p.emit("gpsimd", lambda e, dst=dst, hh=hh: e.dma_start(
                        out=dst[:, 1024 * hh:1024 * (hh + 1)], in_=bcb[:]), inc="gi", dma=True)
            m_img = p.ctr["gi"]

            m_dl_used = [0, 0]
            m_uu_used = 0
            m_dA_used = [0, 0]
            m_uB_used = [0, 0]
            m_crep_used = [0, 0]
            m_y = [0, 0]
            m_y_done = [0, 0]
            it2 = 0
            for mb in range(16):
                for w in range(2):
                    woff = W * w
                    slot = (2 * mb + w) % 2
                    p.wait("sync", "ac", m_dl_used[slot])
                    for mi in range(2):
                        m = 2 * mb + mi
                        p.emit("sync", lambda e, m=m, slot=slot, mi=mi, woff=woff: e.dma_start(
                            out=dl[slot][mi][:], in_=d_sp[:, m, woff:woff + W]), inc="d1", dma=True)
                    mdl = p.ctr["d1"]
                    p.wait("sync", "ve", m_uu_used)
                    for mi in range(2):
                        m = 2 * mb + mi
                        p.emit("sync", lambda e, m=m, mi=mi, woff=woff: e.dma_start(
                            out=uu[mi][:], in_=u_sp[:, m, woff:woff + W]), inc="u1", dma=True)
                    muu = p.ctr["u1"]
                    for n in range(NST):
                        rs = n % 2
                        rsem = "r0" if rs == 0 else "r1"
                        p.wait("gpsimd", "gi", m_img)
                        p.wait("gpsimd", "ve", m_crep_used[rs])
                        p.emit("gpsimd", lambda e, n=n, woff=woff, rs=rs: e.dma_start(
                            out=brep[rs][:], in_=bimg[n:n + 1, woff:woff + W].partition_broadcast(128)), inc=rsem, dma=True)
                        if w == 1:
                            p.emit("gpsimd", lambda e, n=n, woff=woff, rs=rs: e.dma_start(
                                out=crep[rs][:], in_=cimg[n:n + 1, woff + 3072:woff + W].partition_broadcast(128)), inc=rsem, dma=True)
                        mrep_ = p.ctr[rsem]
                        for mi in range(2):
                            m = 2 * mb + mi
                            up = it2 % 2
                            it2 += 1
                            p.wait("scalar", "d1", mdl)
                            p.wait("scalar", "ve", max(m_dA_used[mi], mave))
                            p.emit("scalar", lambda e, m=m, mi=mi, slot=slot: e.activation(
                                out=dA2[mi][:], in_=dl[slot][mi][:], func=ACT.Exp, scale=aval[:, m, n:n + 1]), inc="ac")
                            mexp = p.ctr["ac"]
                            if n == NST - 1 and mi == 1:
                                m_dl_used[slot] = mexp
                            p.wait("vector", "u1", muu)
                            p.wait("vector", rsem, mrep_)
                            p.emit("vector", lambda e, mi=mi, rs=rs, up=up: e.tensor_tensor(
                                out=uB2[up][:], in0=uu[mi][:], in1=brep[rs][:], op=ALU.mult), inc="ve")
                            muB = p.ctr["ve"]
                            if n == NST - 1 and mi == 1:
                                m_uu_used = muB
                            p.wait("vector", "ac", mexp)
                            if w == 0:
                                p.emit("vector", lambda e, m=m, n=n, mi=mi, up=up: e.tensor_tensor_scan(
                                    out=h1b[:], data0=dA2[mi][:], data1=uB2[up][:], initial=0.0,
                                    op0=ALU.mult, op1=ALU.add), inc="ve")
                                m_dA_used[mi] = p.ctr["ve"]
                                m_uB_used[up] = p.ctr["ve"]
                                p.emit("vector", lambda e, m=m, n=n: e.tensor_copy(
                                    out=state[:, m, n:n + 1], in_=h1b[:, W - 1:W]))
                            else:
                                p.emit("vector", lambda e, m=m, n=n, mi=mi, up=up: e.tensor_tensor_scan(
                                    out=h1b[:], data0=dA2[mi][:], data1=uB2[up][:], initial=state[:, m, n:n + 1],
                                    op0=ALU.mult, op1=ALU.add), inc="ve")
                                m_dA_used[mi] = p.ctr["ve"]
                                m_uB_used[up] = p.ctr["ve"]
                                T = trees[mi]
                                hcp = (T["hc0"], T["hc1"])[n % 2]
                                p.emit("vector", lambda e, hcp=hcp, rs=rs: e.tensor_tensor(
                                    out=hcp[:], in0=h1b[:, 3072:W], in1=crep[rs][:], op=ALU.mult), inc="ve")
                                if n % 2 == 1:
                                    if (n // 2) % 2 == 0:
                                        p.emit("vector", lambda e, T=T: e.tensor_tensor(
                                            out=T["s2b"][:], in0=T["hc0"][:], in1=T["hc1"][:], op=ALU.add))
                                    else:
                                        p.emit("vector", lambda e, T=T: e.tensor_tensor(
                                            out=T["hc0"][:], in0=T["hc0"][:], in1=T["hc1"][:], op=ALU.add))
                                        if (n // 4) % 2 == 0:
                                            p.emit("vector", lambda e, T=T: e.tensor_tensor(
                                                out=T["s4b"][:], in0=T["s2b"][:], in1=T["hc0"][:], op=ALU.add))
                                        else:
                                            p.emit("vector", lambda e, T=T: e.tensor_tensor(
                                                out=T["hc0"][:], in0=T["s2b"][:], in1=T["hc0"][:], op=ALU.add))
                                            if n == 7:
                                                p.emit("vector", lambda e, T=T: e.tensor_tensor(
                                                    out=T["s8b"][:], in0=T["s4b"][:], in1=T["hc0"][:], op=ALU.add))
                                            else:
                                                p.wait("vector", ("gy", "gy1")[mi], m_y_done[mi])
                                                p.emit("vector", lambda e, T=T: e.tensor_tensor(
                                                    out=T["hc0"][:], in0=T["s4b"][:], in1=T["hc0"][:], op=ALU.add))
                                                p.emit("vector", lambda e, T=T: e.tensor_tensor(
                                                    out=T["yb"][:], in0=T["s8b"][:], in1=T["hc0"][:], op=ALU.add), inc="ve")
                                                m_y[mi] = p.ctr["ve"]
                            m_crep_used[rs] = p.ctr["ve"]
                    if w == 1:
                        for mi in range(2):
                            m = 2 * mb + mi
                            gsem = ("gy", "gy1")[mi]
                            p.wait("gpsimd", "ve", m_y[mi])
                            p.emit("gpsimd", lambda e, m=m, mi=mi, gsem=gsem: e.dma_start(
                                out=y_sp[:, m, :], in_=trees[mi]["yb"][:]), inc=gsem, dma=True)
                            m_y_done[mi] = p.ctr[gsem]

            p.barrier()

        TH = TB
        with contextlib.ExitStack() as ph:
            sb = lambda n, s, d: ph.enter_context(nc.sbuf_tensor(n, s, d))
            area = sb("area", [128, 44, TH], BF16)
            xn = sb("p_xn", [128, 16, TH], BF16)
            xn2 = xn
            x1 = sb("p_x1", [128, 16, TH], F32)
            fstg = [sb("fstg0", [128, 32, 128], F32), sb("fstg1", [128, 32, 128], F32)]
            istg = [sb("istg0", [128, 16, 128], I32), sb("istg1", [128, 16, 128], I32)]
            wt = [sb("wt0", [128, 44, 128], BF16), sb("wt1", [128, 44, 128], BF16)]
            ytl = sb("ytl", [128, TH], F32)
            gtmp = sb("gtmp", [128, TH], F32)
            gtmpb = sb("gtmpb", [128, TH], F32)
            gt = sb("gt", [128, TH], BF16)
            ut = sb("ut", [128, TH], BF16)
            xct = sb("xct", [128, TH], BF16)
            pstat = sb("p_stat", [1, 4, TH], F32)
            pmrep = sb("p_mrep", [128, TH], F32)
            prrep = sb("p_rrep", [128, TH], F32)
            cs_rep = sb("cs_rep", [128, TH], F32)
            cs2_rep = sb("cs2_rep", [128, TH], F32)
            sgv = sb("sgv", [128, 44], F32)
            suv = sb("suv", [128, 44], F32)
            sdv = sb("sdv", [128, 16], F32)
            sd128 = sb("sd128", [128, 16], F32)
            dp = sb("dp", [128, 32], F32)
            x1r = sb("x1r", [128, TH], F32)
            otile = sb("otile", [128, TH], F32)

            p.emit("sync", lambda e: e.dma_start(out=sgv[:], in_=s_g), inc="sw", dma=True)
            p.emit("sync", lambda e: e.dma_start(out=suv[:], in_=s_u), inc="sw", dma=True)
            p.emit("sync", lambda e: e.dma_start(out=sdv[:], in_=s_d), inc="sw", dma=True)
            p.emit("sync", lambda e: e.dma_start(out=dp[:], in_=dpar), inc="sw", dma=True)
            msetup = p.ctr["sw"]
            p.wait("vector", "sw", msetup)
            p.emit("vector", lambda e: e.tensor_scalar_mul(out=sd128[:], in0=sdv[:], scalar1=-128.0), inc="ve")
            msd = p.ctr["ve"]
            p.wait("scalar", "sw", msetup)

            m_cast = [0, 0]
            m_wused = [0, 0]
            m_evA = [0, 0]
            m_evV = [0, 0]
            m_xn_used = 0
            m_out_done = 0
            m_stats_used = 0
            m_x1r_used = 0
            m_ytl_used = 0
            m_x1sp = 0

            for th in range(2):
                t0 = TH * th
                foff = POFF + TH * th
                p.wait("sync", "te", m_xn_used)
                p.emit("sync", lambda e, foff=foff: e.dma_start(
                    out=xn[:, 0:8, :], in_=xn_sp[:, 0:8, foff:foff + TH]), inc="xa", dma=True)
                p.emit("sync", lambda e, foff=foff: e.dma_start(
                    out=xn[:, 8:16, :], in_=xn_sp[:, 8:16, foff:foff + TH]), inc="xa", dma=True)
                mxn = p.ctr["xa"]
                def z_load(m):
                    sl = m % 2
                    fsem = "f0" if sl == 0 else "f1"
                    p.wait("sync", "ve", m_cast[sl])
                    p.emit("sync", lambda e, m=m, sl=sl: e.dma_start(
                        out=fstg[sl][:, 0:16, :], in_=w_inz[m]), inc=fsem, dma=True)
                    mwl = p.ctr[fsem]
                    p.wait("vector", fsem, mwl)
                    p.wait("vector", "te", m_wused[sl])
                    p.emit("vector", lambda e, sl=sl: e.tensor_copy(out=wt[sl][:, 0:16, :], in_=fstg[sl][:, 0:16, :]), inc="ve")
                    m_cast[sl] = p.ctr["ve"]
                    castm[m] = p.ctr["ve"]
                castm = {}
                z_load(0)
                for m in range(32):
                    sl = m % 2
                    if m + 1 < 32:
                        z_load(m + 1)
                    p.wait("tensor", "ve", castm[m])
                    p.wait("tensor", "xa", mxn)
                    p.wait("tensor", "ac", m_evA[sl])
                    p.wait("tensor", "ve", m_evV[sl])
                    for k in range(16):
                        p.emit("tensor", lambda e, k=k, sl=sl: e.matmul(
                            out=pA[sl][:, :], lhsT=wt[sl][:, k, :], rhs=xn[:, k, :],
                            start=(k == 0), stop=(k == 15)), inc="te" if k == 15 else None)
                    m_wused[sl] = p.ctr["te"]
                    p.wait("scalar", "te", m_wused[sl])
                    p.emit("scalar", lambda e, m=m, sl=sl: e.activation(
                        out=area[:, m, :], in_=pA[sl][:, :], func=ACT.Silu), inc="ac")
                    m_evA[sl] = p.ctr["ac"]
                msz = p.ctr["ac"]
                m_xn_used = p.ctr["te"]
                for m in range(32):
                    p.wait("sync", "ve", m_ytl_used)
                    p.emit("sync", lambda e, m=m, t0=t0: e.dma_start(
                        out=ytl[:, :], in_=y_sp[:, m, t0:t0 + TH]), inc="yx", dma=True)
                    p.emit("sync", lambda e, m=m, foff=foff: e.dma_start(
                        out=xct[:, :], in_=xc_sp[:, m, foff:foff + TH]), inc="yx", dma=True)
                    mxcl = p.ctr["yx"]
                    p.wait("vector", "yx", mxcl)
                    p.wait("vector", "ac", msz)
                    p.emit("vector", lambda e, m=m: e.scalar_tensor_tensor(
                        out=gtmp[:, :], in0=xct[:, :], scalar=dp[:, m:m + 1], in1=ytl[:, :],
                        op0=ALU.mult, op1=ALU.add), inc="ve")
                    m_ytl_used = p.ctr["ve"]
                    p.emit("vector", lambda e, m=m: e.tensor_tensor(
                        out=area[:, m, :], in0=gtmp[:, :], in1=area[:, m, :], op=ALU.mult), inc="ve")
                my2 = p.ctr["ve"]
                def op_load(m):
                    sl = m % 2
                    fsem = "f0" if sl == 0 else "f1"
                    p.wait("sync", "ve", m_cast[sl])
                    p.emit("sync", lambda e, m=m, sl=sl: e.dma_start(
                        out=fstg[sl][:], in_=w_out[m]), inc=fsem, dma=True)
                    mwl = p.ctr[fsem]
                    p.wait("vector", fsem, mwl)
                    p.wait("vector", "te", m_wused[sl])
                    p.emit("vector", lambda e, sl=sl: e.tensor_copy(out=wt[sl][:, 0:32, :], in_=fstg[sl][:]), inc="ve")
                    m_cast[sl] = p.ctr["ve"]
                    castm2[m] = p.ctr["ve"]
                castm2 = {}
                op_load(0)
                for m in range(16):
                    sl = m % 2
                    if m + 1 < 16:
                        op_load(m + 1)
                    p.wait("sync", "ve", m_x1r_used)
                    p.emit("sync", lambda e, m=m, foff=foff: e.dma_start(
                        out=x1r[:, :], in_=xTf[:, m, foff:foff + TH]), inc="xr", dma=True)
                    mxl = p.ctr["xr"]
                    p.wait("tensor", "ve", max(castm2[m], my2))
                    p.wait("tensor", "ac", m_evA[sl])
                    p.wait("tensor", "ve", m_evV[sl])
                    for k in range(32):
                        p.emit("tensor", lambda e, k=k, sl=sl: e.matmul(
                            out=pA[sl][:, :], lhsT=wt[sl][:, k, :], rhs=area[:, k, :],
                            start=(k == 0), stop=(k == 31)), inc="te" if k == 31 else None)
                    m_wused[sl] = p.ctr["te"]
                    p.wait("vector", "te", m_wused[sl])
                    p.wait("vector", "xr", mxl)
                    p.emit("vector", lambda e, m=m, sl=sl: e.tensor_tensor(
                        out=x1[:, m, :], in0=pA[sl][:, :], in1=x1r[:, :], op=ALU.add), inc="ve")
                    m_evV[sl] = p.ctr["ve"]
                    m_x1r_used = p.ctr["ve"]
                mx1 = p.ctr["ve"]
                p.wait("gpsimd", "ve", mx1)
                p.emit("gpsimd", lambda e, th=th: e.dma_start(
                    out=x1sp[th], in_=x1[:, :, :]), inc="gx1", dma=True)
                m_x1sp = p.ctr["gx1"]
                p.wait("tensor", "ve", max(mx1, m_stats_used))
                for k in range(16):
                    p.emit("tensor", lambda e, k=k: e.matmul(
                        out=pst0[0:1, 0:TH], lhsT=onesf[:], rhs=x1[:, k, :],
                        start=(k == 0), stop=(k == 15)), inc="te" if k == 15 else None)
                for sp2 in range(2):
                    c0 = 256 * sp2
                    sl = sp2 % 2
                    p.wait("scalar", "ve", max(mx1, m_cast[sl]))
                    p.emit("scalar", lambda e, c0=c0, sl=sl: e.activation(
                        out=fstg[sl][:].rearrange("p a b -> p (a b)").rearrange("p (a c) -> p a c", a=16),
                        in_=x1[:, :, c0:c0 + 256], func=ACT.Square), inc="ac")
                    msq2 = p.ctr["ac"]
                    p.wait("tensor", "ac", msq2)
                    for k in range(16):
                        p.emit("tensor", lambda e, k=k, c0=c0, sl=sl: e.matmul(
                            out=pst1[0:1, c0:c0 + 256],
                            lhsT=onesf[:],
                            rhs=fstg[sl][:].rearrange("p a b -> p (a b)").rearrange("p (a c) -> p a c", a=16)[:, k, :],
                            start=(k == 0), stop=(k == 15)), inc="te" if k == 15 else None)
                    m_wused[sl] = max(m_wused[sl], p.ctr["te"])
                msx2 = p.ctr["te"]
                p.wait("vector", "te", msx2)
                p.emit("vector", lambda e: e.tensor_scalar_mul(out=pstat[0:1, 0, :], in0=pst0[0:1, 0:TH], scalar1=1.0 / D))
                p.emit("vector", lambda e: e.tensor_scalar_mul(out=pstat[0:1, 1, :], in0=pst1[0:1, 0:TH], scalar1=1.0 / D))
                p.emit("vector", lambda e: e.tensor_tensor(out=pstat[0:1, 2, :], in0=pstat[0:1, 0, :], in1=pstat[0:1, 0, :], op=ALU.mult))
                p.emit("vector", lambda e: e.tensor_tensor(out=pstat[0:1, 2, :], in0=pstat[0:1, 1, :], in1=pstat[0:1, 2, :], op=ALU.subtract))
                p.emit("vector", lambda e: e.tensor_scalar_add(out=pstat[0:1, 2, :], in0=pstat[0:1, 2, :], scalar1=EPS), inc="ve")
                mvar = p.ctr["ve"]
                m_stats_used = mvar
                p.wait("scalar", "ve", mvar)
                p.emit("scalar", lambda e: e.activation(out=pstat[0:1, 2, :], in_=pstat[0:1, 2, :], func=ACT.Sqrt), inc="ac")
                msqr = p.ctr["ac"]
                p.wait("vector", "ac", msqr)
                p.emit("vector", lambda e: e.reciprocal(out=pstat[0:1, 2, :], in_=pstat[0:1, 2, :]), inc="ve")
                mrs = p.ctr["ve"]
                p.wait("gpsimd", "ve", mrs)
                p.emit("gpsimd", lambda e: e.dma_start(out=srow[0:1, :], in_=pstat[0:1, 0, :]), inc="gst", dma=True)
                p.emit("gpsimd", lambda e: e.dma_start(out=srow[1:2, :], in_=pstat[0:1, 2, :]), inc="gst", dma=True)
                mb1 = p.ctr["gst"]
                p.wait("gpsimd", "gst", mb1)
                p.emit("gpsimd", lambda e: e.dma_start(out=pmrep[:, :], in_=srow[0:1, :].partition_broadcast(128)), inc="gst", dma=True)
                p.emit("gpsimd", lambda e: e.dma_start(out=prrep[:, :], in_=srow[1:2, :].partition_broadcast(128)), inc="gst", dma=True)
                mb2 = p.ctr["gst"]
                p.wait("vector", "gst", mb2)
                p.wait("vector", "te", max(msx2, m_xn_used))
                for k in range(16):
                    p.emit("vector", lambda e, k=k: e.tensor_tensor(
                        out=x1[:, k, :], in0=x1[:, k, :], in1=pmrep[:, :], op=ALU.subtract))
                    p.emit("vector", lambda e, k=k: e.tensor_tensor(
                        out=xn2[:, k, :], in0=x1[:, k, :], in1=prrep[:, :], op=ALU.mult))
                p.emit("vector", lambda e: e.engine_nop(), inc="ve")
                mxn2 = p.ctr["ve"]
                p.wait("tensor", "ve", max(mxn2, m_stats_used))
                for k in range(16):
                    p.emit("tensor", lambda e, k=k: e.matmul(
                        out=pst0[0:1, 0:TH], lhsT=onesb[:], rhs=xn2[:, k, :],
                        start=(k == 0), stop=(k == 15)), inc="te" if k == 15 else None)
                mcs = p.ctr["te"]
                p.wait("vector", "te", mcs)
                p.emit("vector", lambda e: e.tensor_scalar_mul(out=pstat[0:1, 3, :], in0=pst0[0:1, 0:TH], scalar1=128.0), inc="ve")
                mcsr = p.ctr["ve"]
                m_stats_used = mcsr
                p.wait("gpsimd", "ve", mcsr)
                p.emit("gpsimd", lambda e: e.dma_start(out=srow[0:1, :], in_=pstat[0:1, 3, :]), inc="gst", dma=True)
                mb3 = p.ctr["gst"]
                p.wait("gpsimd", "gst", mb3)
                p.emit("gpsimd", lambda e: e.dma_start(out=cs_rep[:, :], in_=srow[0:1, :].partition_broadcast(128)), inc="gst", dma=True)
                mb4 = p.ctr["gst"]
                m_gt_used = 0
                castf = {}

                def fu_load(it):
                    m, gi = it // 2, it % 2
                    wq = (wq_g, wq_u)[gi]
                    sl = it % 2
                    isem = "i0" if sl == 0 else "i1"
                    p.wait("sync", "ve", m_cast[sl])
                    p.emit("sync", lambda e, m=m, sl=sl, wq=wq: e.dma_start(
                        out=istg[sl][:, 0:16, :], in_=wq[m]), inc=isem, dma=True)
                    mwl = p.ctr[isem]
                    p.wait("vector", isem, mwl)
                    p.wait("vector", "te", m_wused[sl])
                    p.emit("vector", lambda e, sl=sl: e.tensor_copy(out=wt[sl][:, 0:16, :], in_=istg[sl][:, 0:16, :]), inc="ve")
                    m_cast[sl] = p.ctr["ve"]
                    castf[it] = p.ctr["ve"]
                fu_load(0)
                fu_load(1)
                for m in range(44):
                    for gi in range(2):
                        it = 2 * m + gi
                        sv = (sgv, suv)[gi]
                        fn = (ACT.Silu, ACT.Identity)[gi]
                        sl = it % 2
                        gtp = (gtmp, gtmpb)[gi]
                        p.wait("tensor", "ve", max(castf[it], mxn2))
                        p.wait("tensor", "ac", m_evA[sl])
                        p.wait("tensor", "ve", m_evV[sl])
                        for k in range(16):
                            p.emit("tensor", lambda e, k=k, sl=sl: e.matmul(
                                out=pA[sl][:, :], lhsT=wt[sl][:, k, :], rhs=xn2[:, k, :],
                                start=(k == 0), stop=(k == 15)), inc="te" if k == 15 else None)
                        m_wused[sl] = p.ctr["te"]
                        if it + 2 < 88:
                            fu_load(it + 2)
                        p.wait("vector", "te", m_wused[sl])
                        p.wait("vector", "gst", mb4)
                        p.wait("vector", "ac", m_gt_used)
                        p.emit("vector", lambda e, sl=sl, gtp=gtp: e.tensor_tensor(
                            out=gtp[:, :], in0=pA[sl][:, :], in1=cs_rep[:, :], op=ALU.subtract), inc="ve")
                        m_evV[sl] = p.ctr["ve"]
                        mev = p.ctr["ve"]
                        p.wait("scalar", "ve", mev)
                        if gi == 0:
                            p.emit("scalar", lambda e, m=m, sv=sv, fn=fn, gtp=gtp: e.activation(
                                out=gt[:, :], in_=gtp[:, :], func=fn, scale=sv[:, m:m + 1]), inc="ac")
                        else:
                            p.emit("scalar", lambda e, m=m, sv=sv, fn=fn, gtp=gtp: e.activation(
                                out=ut[:, :], in_=gtp[:, :], func=fn, scale=sv[:, m:m + 1]), inc="ac")
                    mgu = p.ctr["ac"]
                    p.wait("vector", "ac", mgu)
                    p.emit("vector", lambda e, m=m: e.tensor_tensor(
                        out=area[:, m, :], in0=gt[:, :], in1=ut[:, :], op=ALU.mult), inc="ve")
                    m_gt_used = mgu
                mguall = p.ctr["ve"]
                p.wait("tensor", "ve", max(mguall, m_stats_used))
                for k in range(44):
                    p.emit("tensor", lambda e, k=k: e.matmul(
                        out=pst0[0:1, 0:TH], lhsT=onesb[:], rhs=area[:, k, :],
                        start=(k == 0), stop=(k == 43)), inc="te" if k == 43 else None)
                mcs2 = p.ctr["te"]
                p.wait("vector", "te", mcs2)
                p.emit("vector", lambda e: e.tensor_copy(out=pstat[0:1, 3, :], in_=pst0[0:1, 0:TH]), inc="ve")
                mcs2v = p.ctr["ve"]
                m_stats_used = mcs2v
                p.wait("gpsimd", "ve", mcs2v)
                p.emit("gpsimd", lambda e: e.dma_start(out=srow[1:2, :], in_=pstat[0:1, 3, :]), inc="gst", dma=True)
                mb5 = p.ctr["gst"]
                p.wait("gpsimd", "gst", mb5)
                p.emit("gpsimd", lambda e: e.dma_start(out=cs2_rep[:, :], in_=srow[1:2, :].partition_broadcast(128)), inc="gst", dma=True)
                mb6 = p.ctr["gst"]
                castd = {}

                def dn_load(m):
                    sl = m % 2
                    isem = "i0" if sl == 0 else "i1"
                    for gof, gn in ((0, 16), (16, 16), (32, 12)):
                        p.wait("sync", "ve", m_cast[sl])
                        p.emit("sync", lambda e, m=m, sl=sl, gof=gof, gn=gn: e.dma_start(
                            out=istg[sl][:, 0:gn, :], in_=wq_d[m, :, gof:gof + gn, :]), inc=isem, dma=True)
                        mwl = p.ctr[isem]
                        p.wait("vector", isem, mwl)
                        if gof == 0:
                            p.wait("vector", "te", m_wused[sl])
                        p.emit("vector", lambda e, sl=sl, gof=gof, gn=gn: e.tensor_copy(
                            out=wt[sl][:, gof:gof + gn, :], in_=istg[sl][:, 0:gn, :]), inc="ve")
                        m_cast[sl] = p.ctr["ve"]
                    castd[m] = p.ctr["ve"]
                dn_load(0)
                for m in range(16):
                    sl = m % 2
                    if m + 1 < 16:
                        dn_load(m + 1)
                    p.wait("sync", "ve", m_x1r_used)
                    p.wait("sync", "gx1", m_x1sp)
                    p.emit("sync", lambda e, m=m, th=th: e.dma_start(
                        out=x1r[:, :], in_=x1sp[th, :, m, :]), inc="xr", dma=True)
                    mx1l = p.ctr["xr"]
                    p.wait("tensor", "ve", castd[m])
                    p.wait("tensor", "ac", m_evA[sl])
                    p.wait("tensor", "ve", m_evV[sl])
                    for k in range(44):
                        p.emit("tensor", lambda e, k=k, sl=sl: e.matmul(
                            out=pA[sl][:, :], lhsT=wt[sl][:, k, :], rhs=area[:, k, :],
                            start=(k == 0), stop=(k == 43)), inc="te" if k == 43 else None)
                    m_wused[sl] = p.ctr["te"]
                    p.wait("vector", "te", m_wused[sl])
                    p.wait("vector", "gst", mb6)
                    p.wait("vector", "xr", mx1l)
                    p.wait("vector", "ve", msd)
                    p.wait("vector", "go", m_out_done)
                    p.emit("vector", lambda e, m=m: e.scalar_tensor_tensor(
                        out=gtmp[:, :], in0=cs2_rep[:, :], scalar=sd128[:, m:m + 1], in1=x1r[:, :],
                        op0=ALU.mult, op1=ALU.add))
                    p.emit("vector", lambda e, m=m, sl=sl: e.scalar_tensor_tensor(
                        out=otile[:, :], in0=pA[sl][:, :], scalar=sdv[:, m:m + 1], in1=gtmp[:, :],
                        op0=ALU.mult, op1=ALU.add), inc="ve")
                    m_evV[sl] = p.ctr["ve"]
                    m_x1r_used = p.ctr["ve"]
                    mot = p.ctr["ve"]
                    p.wait("gpsimd", "ve", mot)
                    p.emit("gpsimd", lambda e, m=m, t0=t0: e.dma_start(
                        out=o_T[:, m, t0:t0 + TH], in_=otile[:, :]), inc="go", dma=True)
                    m_out_done = p.ctr["go"]
            for nmm in ("go", "gx1", "gst"):
                p.wait("gpsimd", nmm, p.ctr[nmm])
                p.wait("sync", nmm, p.ctr[nmm])
            p.replay()
    return nc



def _chunk(a, p=128):
    n, t = a.shape
    c = n // p
    return np.ascontiguousarray(a.reshape(c, p, t).transpose(1, 0, 2))


def _unchunk(a):
    p, c, t = a.shape
    return np.ascontiguousarray(a.transpose(1, 0, 2).reshape(c * p, t))


def _vec_chunk(v, p=128):
    return np.ascontiguousarray(v.reshape(-1, p).T)


def kernel(L0_attn_hidden_states, L0_attn_in_proj_w, L0_attn_conv1d_w,
           L0_attn_x_proj_w, L0_attn_dt_proj_w, L0_attn_dt_proj_bias,
           L0_attn_A_log, L0_attn_D_param, L0_attn_out_proj_w,
           L0_ffn_wq_gate, L0_ffn_s_gate, L0_ffn_wq_up, L0_ffn_s_up,
           L0_ffn_wq_down, L0_ffn_s_down):
    X = np.asarray(L0_attn_hidden_states)[0]
    XT = np.ascontiguousarray(X.T)

    w_inx0 = _chunk(np.ascontiguousarray(np.asarray(L0_attn_in_proj_w)[:DIN].T))
    w_inx = np.ascontiguousarray(w_inx0.reshape(128, 16, 16, 256).transpose(2, 0, 1, 3))
    convw = _chunk(np.asarray(L0_attn_conv1d_w))
    w_xp = _chunk(np.ascontiguousarray(np.asarray(L0_attn_x_proj_w).T))
    w_dt = np.ascontiguousarray(np.asarray(L0_attn_dt_proj_w).T)
    dt_b = _vec_chunk(np.asarray(L0_attn_dt_proj_bias))
    alog = _chunk(np.asarray(L0_attn_A_log).reshape(DIN, NST))
    w_inz0 = _chunk(np.ascontiguousarray(np.asarray(L0_attn_in_proj_w)[DIN:].T))
    w_inz = np.ascontiguousarray(w_inz0.reshape(128, 16, 32, 128).transpose(2, 0, 1, 3))
    w_out0 = _chunk(np.ascontiguousarray(np.asarray(L0_attn_out_proj_w).T))
    w_out = np.ascontiguousarray(w_out0.reshape(128, 32, 16, 128).transpose(2, 0, 1, 3))
    wq_g0 = _chunk(np.ascontiguousarray(np.asarray(L0_ffn_wq_gate).T))
    wq_g = np.ascontiguousarray(wq_g0.reshape(128, 16, 44, 128).transpose(2, 0, 1, 3))
    wq_u0 = _chunk(np.ascontiguousarray(np.asarray(L0_ffn_wq_up).T))
    wq_u = np.ascontiguousarray(wq_u0.reshape(128, 16, 44, 128).transpose(2, 0, 1, 3))
    wq_d0 = _chunk(np.ascontiguousarray(np.asarray(L0_ffn_wq_down).T))
    wq_d = np.ascontiguousarray(wq_d0.reshape(128, 44, 16, 128).transpose(2, 0, 1, 3))
    s_g = _vec_chunk(np.asarray(L0_ffn_s_gate))
    s_u = _vec_chunk(np.asarray(L0_ffn_s_up))
    s_d = _vec_chunk(np.asarray(L0_ffn_s_down))
    dpar = _vec_chunk(np.asarray(L0_attn_D_param))

    nc = build_fused()
    ins = []
    for c in range(NC):
        pad = (NC - 1 - c) * TC
        fr = np.zeros((D, SEQ), np.float32)
        fr[:, pad:] = XT[:, :SEQ - pad] if pad else XT
        ins.append({
            "xTf": _chunk(fr),
            "w_inx": w_inx, "convw": convw, "w_xp": w_xp,
            "w_dt": w_dt, "dt_b": dt_b, "alog_i": alog,
            "w_inz": w_inz, "w_out": w_out,
            "wq_g": wq_g, "wq_u": wq_u, "wq_d": wq_d,
            "s_g": s_g, "s_u": s_u, "s_d": s_d, "dpar": dpar,
        })
    r = run_bass_kernel_spmd(nc, ins, core_ids=list(range(NC)))
    oT = np.concatenate([_unchunk(r.results[c]["o_T"]) for c in range(NC)], axis=1)
    return np.ascontiguousarray(oT.T).reshape(1, SEQ, D).astype(np.float32)
